# version 28
# baseline (speedup 1.0000x reference)
import numpy as np
import ml_dtypes
import concourse.bass as bass
import concourse.mybir as mybir
from concourse.bass_utils import run_bass_kernel_spmd

F32 = mybir.dt.float32
BF16 = mybir.dt.bfloat16
U8 = mybir.dt.uint8
AF = mybir.ActivationFunctionType
ALU = mybir.AluOpType

P = 128
T = 2048
D = 1024
KC = 8
NH = 16
DH = 64
FF = 4096
FC = 32
NTT = 4
TTW = 512
NQB = 2
QBW = 1024
NMC = 16
VROW = NH * 65
RANK = 16

EXP_SCALE = 0.125
EXP_SHIFT = -4.0
LN_EPS = 1e-5
PHASES = 4


class Arena:

    def __init__(self, nc, total_bytes):
        self.total = total_bytes
        self.h8 = nc.alloc_sbuf_tensor("arena", [P, total_bytes], U8)
        self.hbf = self.h8.bitcast(BF16)
        self.hf32 = self.h8.bitcast(F32)
        self.free_list = [(0, total_bytes)]
        self.live = {}

    def alloc(self, name, nbytes):
        nbytes = (nbytes + 127) & ~127
        best = None
        for i, (off, sz) in enumerate(self.free_list):
            if sz >= nbytes and (best is None or sz < self.free_list[best][1]):
                best = i
        if best is None:
            raise MemoryError(f"arena: cannot alloc {name} ({nbytes}B); free={self.free_list}")
        off, sz = self.free_list[best]
        self.free_list[best] = (off + nbytes, sz - nbytes)
        if self.free_list[best][1] == 0:
            del self.free_list[best]
        self.live[name] = (off, nbytes)
        return off

    def free(self, name):
        off, sz = self.live.pop(name)
        self.free_list.append((off, sz))
        self.free_list.sort()
        merged = []
        for o, s in self.free_list:
            if merged and merged[-1][0] + merged[-1][1] == o:
                merged[-1] = (merged[-1][0], merged[-1][1] + s)
            else:
                merged.append((o, s))
        self.free_list = [tuple(m) for m in merged]

    def bf(self, name, p, n, row_off=0, col_off=0):
        off = self.live[name][0]
        assert off % 2 == 0
        return self.hbf[row_off:row_off + p, off // 2 + col_off: off // 2 + col_off + n]

    def f32(self, name, p, n, row_off=0, col_off=0):
        off = self.live[name][0]
        assert off % 4 == 0
        return self.hf32[row_off:row_off + p, off // 4 + col_off: off // 4 + col_off + n]


def build_core_program():
    nc = bass.Bass("TRN2", target_bir_lowering=False, detect_race_conditions=False)

    xR = nc.dram_tensor("xR", [P, KC * T], F32, kind="ExternalInput")
    wqL = nc.dram_tensor("wqL", [P, KC * D], F32, kind="ExternalInput")
    wkL = nc.dram_tensor("wkL", [P, KC * D], F32, kind="ExternalInput")
    wvL = nc.dram_tensor("wvL", [P, KC * D], F32, kind="ExternalInput")
    woPL = nc.dram_tensor("woPL", [P, KC * D], F32, kind="ExternalInput")
    wiL = nc.dram_tensor("wiL", [P, KC * FF], F32, kind="ExternalInput")
    woL = nc.dram_tensor("woL", [P, KC * FF], F32, kind="ExternalInput")
    a01 = nc.dram_tensor("a01", [P, 2 * 128], F32, kind="ExternalInput")
    b0T = nc.dram_tensor("b0T", [RANK, FF], F32, kind="ExternalInput")
    b1T = nc.dram_tensor("b1T", [RANK, FF], F32, kind="ExternalInput")
    bias4 = nc.dram_tensor("bias4", [P, 32], F32, kind="ExternalInput")
    bvrow = nc.dram_tensor("bvrow", [1, D], F32, kind="ExternalInput")
    tkv = nc.dram_tensor("tkv", [P, 2], F32, kind="ExternalInput")
    tabs = nc.dram_tensor("tabs", [P, 2 * T], F32, kind="ExternalInput")
    swp = nc.dram_tensor("swp", [P, P], F32, kind="ExternalInput")
    onesrf = nc.dram_tensor("onesrf", [1, P], F32, kind="ExternalInput")
    sel16 = nc.dram_tensor("sel16", [RANK, KC * P], F32, kind="ExternalInput")
    yT = nc.dram_tensor("yT", [D, T], F32, kind="ExternalOutput")
    xn_hbm = nc.dram_tensor("xn_hbm", [P, KC * T], BF16, kind="Internal")
    at_hbm = nc.dram_tensor("at_hbm", [P, KC * T], BF16, kind="Internal")
    rec_hbm = nc.dram_tensor("rec_hbm", [RANK, T], BF16, kind="Internal")
    h_hbm = nc.dram_tensor("h_hbm", [P, FC * T], BF16, kind="Internal")

    A = Arena(nc, (nc.sbuf_bytes_remaining - 1024) & ~127)

    A.alloc("bias4", 32 * 4)
    A.alloc("tkv", 2 * 4)
    A.alloc("cst", 8 * 4)
    A.alloc("swp", P * 2)
    A.alloc("onesrf", P * 4)
    A.alloc("onesrb", P * 2)
    A.alloc("onescb", 128)
    A.alloc("onescf", 128)
    A.alloc("bvrow", D * 2)
    A.alloc("tabs", 2 * T * 2)
    A.alloc("sel16", KC * P * 2)
    A.alloc("xn", KC * T * 2)

    from contextlib import ExitStack
    with ExitStack() as stk:
        names = ["s_cst", "s_w0", "s_w1", "s_wi", "s_wop", "s_wo0", "s_wo1",
                 "s_cld", "s_xt0", "s_xt1", "s_xnin", "s_att0", "s_att1",
                 "s_xc0", "s_xc1", "s_xnst", "s_ast0", "s_ast1", "s_yst0",
                 "s_yst1", "s_pe", "s_sc", "s_ve", "s_rec", "s_sq",
                 "s_pei", "s_sci", "s_vei", "s_wis0", "s_wis1",
                 "s_hst0", "s_hst1", "s_hld", "s_pe2", "s_wos"]
        S = {n: stk.enter_context(nc.semaphore(n)) for n in names}
        blk = stk.enter_context(nc.Block())

        env = dict(
            xR=xR, wqL=wqL, wkL=wkL, wvL=wvL, woPL=woPL, wiL=wiL, woL=woL,
            a01=a01, b0T=b0T, b1T=b1T, bias4=bias4, bvrow=bvrow, tkv=tkv,
            tabs=tabs, swp=swp, onesrf=onesrf, sel16=sel16, yT=yT, xn_hbm=xn_hbm,
            at_hbm=at_hbm, rec_hbm=rec_hbm,
            s_cst=S["s_cst"], s_wi=S["s_wi"], s_wop=S["s_wop"],
            s_cld=S["s_cld"], s_xnin=S["s_xnin"], s_xnst=S["s_xnst"],
            s_pe=S["s_pe"], s_sc=S["s_sc"], s_ve=S["s_ve"],
            s_xt=[S["s_xt0"], S["s_xt1"]], s_w=[S["s_w0"], S["s_w1"]],
            s_wo=[S["s_wo0"], S["s_wo1"]], s_att=[S["s_att0"], S["s_att1"]],
            s_xc=[S["s_xc0"], S["s_xc1"]], s_ast=[S["s_ast0"], S["s_ast1"]],
            s_yst=[S["s_yst0"], S["s_yst1"]], s_rec=S["s_rec"], s_sq=S["s_sq"],
            s_pei=S["s_pei"], s_sci=S["s_sci"], s_vei=S["s_vei"],
            s_wis=[S["s_wis0"], S["s_wis1"]],
            s_hst=[S["s_hst0"], S["s_hst1"]], s_hld=S["s_hld"],
            s_pe2=S["s_pe2"], s_wos=S["s_wos"],
            h_hbm=h_hbm,
        )
        C = dict(pe=0, sc=0, ve=0)
        with nc.allow_low_precision(reason="bf16 working precision by design"):
            _build_body(nc, blk, A, C, env)
    return nc


def _build_body(nc, blk, A, C, env):
    xR = env["xR"]; wqL = env["wqL"]; wkL = env["wkL"]; wvL = env["wvL"]
    woPL = env["woPL"]; wiL = env["wiL"]; woL = env["woL"]
    a01 = env["a01"]; b0T = env["b0T"]; b1T = env["b1T"]
    bias4 = env["bias4"]; bvrow = env["bvrow"]; tkv = env["tkv"]
    tabs = env["tabs"]; swp = env["swp"]; onesrf = env["onesrf"]; sel16 = env["sel16"]
    yT = env["yT"]; xn_hbm = env["xn_hbm"]; at_hbm = env["at_hbm"]; rec_hbm = env["rec_hbm"]
    s_cst = env["s_cst"]; s_w = env["s_w"]; s_wi = env["s_wi"]; s_wop = env["s_wop"]
    s_wo = env["s_wo"]; s_cld = env["s_cld"]; s_xt = env["s_xt"]; s_xnin = env["s_xnin"]
    s_att = env["s_att"]; s_xc = env["s_xc"]; s_xnst = env["s_xnst"]
    s_ast = env["s_ast"]; s_yst = env["s_yst"]
    s_pe = env["s_pe"]; s_sc = env["s_sc"]; s_ve = env["s_ve"]; s_rec = env["s_rec"]
    s_sq = env["s_sq"]
    s_pei = env["s_pei"]; s_sci = env["s_sci"]; s_vei = env["s_vei"]
    s_wis = env["s_wis"]; s_hst = env["s_hst"]; s_hld = env["s_hld"]
    h_hbm = env["h_hbm"]; s_pe2 = env["s_pe2"]; s_wos = env["s_wos"]

    A.alloc("xt0", TTW * KC * 4)
    A.alloc("xt1", TTW * KC * 4)
    A.alloc("xt2", TTW * KC * 4)
    A.alloc("xsq0", TTW * KC * 2)
    A.alloc("xsq1", TTW * KC * 2)
    A.alloc("tmpA0", TTW * 4)
    A.alloc("tmpA1", TTW * 4)
    A.alloc("mrow0", TTW * 4, )
    A.alloc("mrow1", TTW * 4)
    A.alloc("rrow0", TTW * 4)
    A.alloc("rrow1", TTW * 4)
    A.alloc("exrow0", TTW * 4)
    A.alloc("exrow1", TTW * 4)
    A.alloc("var0", TTW * 4)
    A.alloc("var1", TTW * 4)
    A.alloc("w0", KC * D * 2)
    A.alloc("w1", KC * D * 2)
    A.alloc("qb0", QBW * 2)
    A.alloc("qb1", QBW * 2)
    A.alloc("t1s0", QBW * 2)
    A.alloc("t1s1", QBW * 2)
    A.alloc("t2s0", QBW * 2)
    A.alloc("t2s1", QBW * 2)

    def xt(i):
        return A.f32(f"xt{i % 3}", P, KC * TTW)

    @blk.gpsimd
    def _(g):
        g.memset(A.f32("cst", P, 1, col_off=0), EXP_SCALE)
        g.memset(A.f32("cst", P, 1, col_off=1), EXP_SHIFT)
        g.memset(A.f32("cst", P, 1, col_off=2), LN_EPS)
        g.memset(A.f32("cst", P, 1, col_off=3), -0.5)
        g.memset(A.f32("cst", P, 1, col_off=4), -1.0)
        g.memset(A.bf("onesrb", 1, P), 1.0)
        g.memset(A.f32("onescf", P, 1), 1.0)

        g.memset(A.bf("onescb", P, 1), 1.0)
        g.dma_start(out=A.f32("bias4", P, 32), in_=bias4[:, :]).then_inc(s_cst, 16)
        g.dma_start(out=A.f32("tkv", P, 2), in_=tkv[:, :]).then_inc(s_cst, 16)
        g.dma_start(out=A.bf("swp", P, P), in_=swp[:, :]).then_inc(s_cst, 16)
        g.dma_start(out=A.f32("onesrf", 1, P), in_=onesrf[:, :]).then_inc(s_cst, 16)
        g.memset(A.bf("sel16", P, KC * P), 0.0)
        g.dma_start(out=A.bf("sel16", RANK, KC * P), in_=sel16[:, :]).then_inc(s_cst, 16)
        g.dma_start(out=A.bf("bvrow", 1, D), in_=bvrow[:, :]).then_inc(s_cst, 16)
        g.dma_start(out=A.bf("tabs", P, 2 * T), in_=tabs[:, :]).then_inc(s_cst, 16)
        g.dma_start(out=A.bf("w0", P, KC * D), in_=wqL[:, :]).then_inc(s_w[0], 16)
        g.dma_start(out=A.bf("w1", P, KC * D), in_=wkL[:, :]).then_inc(s_w[1], 16)

    pe0, sc0, ve0 = C["pe"], C["sc"], C["ve"]

    @blk.sync
    def _(sy):
        for tt in range(NTT):
            if tt >= 3:
                sy.wait_ge(s_ve, ve0 + [3, 5, 7, 8][tt - 3])
            src = bass.AP(xR, tt * TTW, [[KC * T, P], [T, KC], [1, TTW]])
            sy.dma_start(out=xt(tt), in_=src).then_inc(s_xt[tt % 2], 16)

    def sc_rstd(tt):
        return sc0 + tt + 1

    @blk.scalar
    def _(sc):
        def square(tt):
            sc.wait_ge(s_xt[tt % 2], 16 * (tt // 2 + 1))
            if tt >= 2:
                sc.wait_ge(s_pe, pe0 + 2 * (tt - 2) + 1)
            sc.activation(A.bf(f"xsq{tt % 2}", P, KC * TTW),
                          A.f32(f"xt{tt % 3}", P, KC * TTW),
                          AF.Square).then_inc(s_sq, 1)

        def rstd(tt):
            sc.wait_ge(s_ve, ve0 + [1, 2, 4, 6][tt])
            var = A.f32(f"var{tt % 2}", 1, TTW)
            sc.activation(var, var, AF.Ln, bias=A.f32("cst", 1, 1, col_off=2))
            sc.activation(A.f32(f"rrow{tt % 2}", 1, TTW), var, AF.Exp,
                          scale=A.f32("cst", 1, 1, col_off=3)).then_inc(s_sc, 1)

        square(0); square(1); rstd(0); square(2)
        rstd(1); square(3); rstd(2); rstd(3)
    C["sc"] = sc0 + 4

    with (
        nc.psum_tensor("ps_sum0", [1, TTW], F32) as ps_sum0,
        nc.psum_tensor("ps_sum1", [1, TTW], F32) as ps_sum1,
        nc.psum_tensor("ps_sq0", [1, TTW], F32) as ps_sq0,
        nc.psum_tensor("ps_sq1", [1, TTW], F32) as ps_sq1,
        nc.psum_tensor("ps_m0", [P, TTW], F32) as ps_m0,
        nc.psum_tensor("ps_m1", [P, TTW], F32) as ps_m1,
        nc.psum_tensor("ps_r0", [P, TTW], F32) as ps_r0,
        nc.psum_tensor("ps_r1", [P, TTW], F32) as ps_r1,
    ):
        ps_sum = [ps_sum0, ps_sum1]
        ps_sq = [ps_sq0, ps_sq1]
        ps_m = [ps_m0, ps_m1]
        ps_r = [ps_r0, ps_r1]

        @blk.tensor
        def _(t):
            t.wait_ge(s_cst, 16 * 7)
            for tt in range(NTT):
                if tt >= 2:
                    t.wait_ge(s_ve, ve0 + [1, 2, 4, 6][tt - 2])
                onesc = A.bf("onescb", P, 1)
                t.wait_ge(s_xt[tt % 2], 16 * (tt // 2 + 1))
                for kc in range(KC):
                    t.matmul(ps_sum[tt % 2][0:1, :], A.f32("onescf", P, 1),
                             A.f32(f"xt{tt % 3}", P, TTW, col_off=kc * TTW),
                             start=(kc == 0), stop=(kc == KC - 1))
                t.wait_ge(s_sq, tt + 1)
                for kc in range(KC):
                    ins = t.matmul(ps_sq[tt % 2][0:1, :], onesc,
                                   A.bf(f"xsq{tt % 2}", P, TTW, col_off=kc * TTW),
                                   start=(kc == 0), stop=(kc == KC - 1))
                ins.then_inc(s_pe, 1)
                t.wait_ge(s_sc, sc_rstd(tt))
                if tt >= 2:
                    t.wait_ge(s_ve, ve0 + [3, 5, 7, 8][tt - 2])
                onr = A.f32("onesrf", 1, P)
                t.matmul(ps_m[tt % 2][:, :], onr, A.f32(f"mrow{tt % 2}", 1, TTW),
                         start=True, stop=True)
                t.matmul(ps_r[tt % 2][:, :], onr, A.f32(f"rrow{tt % 2}", 1, TTW),
                         start=True, stop=True).then_inc(s_pe, 1)
        C["pe"] = pe0 + 8

        VE_OPS = [1, 2, 4, 6]
        VE_XN = [3, 5, 7, 8]

        @blk.vector
        def _(v):
            def stats_ops(tt):
                v.wait_ge(s_pe, pe0 + 2 * tt + 1)
                mr = A.f32(f"mrow{tt % 2}", 1, TTW)
                ex = A.f32(f"exrow{tt % 2}", 1, TTW)
                var = A.f32(f"var{tt % 2}", 1, TTW)
                v.tensor_scalar_mul(mr, ps_sum[tt % 2][0:1, :], 1.0 / D)
                v.tensor_scalar_mul(ex, ps_sq[tt % 2][0:1, :], 1.0 / D)
                v.tensor_tensor(var, mr, mr, ALU.mult)
                v.tensor_tensor(var, ex, var, ALU.subtract).then_inc(s_ve, 1)

            def xn_ops(tt):
                v.wait_ge(s_pe, pe0 + 2 * tt + 2)
                v.wait_ge(s_xt[tt % 2], 16 * (tt // 2 + 1))
                for kc in range(KC):
                    tmp = A.f32(f"tmpA{kc % 2}", P, TTW)
                    v.tensor_tensor(tmp, A.f32(f"xt{tt % 3}", P, TTW, col_off=kc * TTW),
                                    ps_m[tt % 2][:, :], ALU.subtract)
                    ins = v.tensor_tensor(A.bf("xn", P, TTW, col_off=kc * T + tt * TTW),
                                          tmp, ps_r[tt % 2][:, :], ALU.mult)
                ins.then_inc(s_ve, 1)

            stats_ops(0); stats_ops(1); xn_ops(0); stats_ops(2)
            xn_ops(1); stats_ops(3); xn_ops(2); xn_ops(3)
        C["ve"] = ve0 + 8

    @blk.sync
    def _(sy):
        sy.wait_ge(s_ve, ve0 + 8)
        sy.dma_start(out=xn_hbm[:, :], in_=A.bf("xn", P, KC * T)).then_inc(s_xnst, 16)

    if PHASES < 2:
        @blk.sync
        def _(sy):
            sy.wait_ge(s_xnst, 16)
        return
    A.free("xt0"); A.free("xt1"); A.free("xt2")
    A.free("xsq0"); A.free("xsq1"); A.free("tmpA0"); A.free("tmpA1")
    for nm in ["mrow0", "mrow1", "rrow0", "rrow1", "exrow0", "exrow1", "var0", "var1"]:
        A.free(nm)
    A.alloc("qrope", KC * T * 2)
    A.alloc("krope", KC * T * 2)
    A.alloc("vst", NMC * VROW * 2)

    pe0, sc0, ve0 = C["pe"], C["sc"], C["ve"]

    @blk.gpsimd
    def _(g):
        g.wait_ge(s_ve, ve0)
        g.wait_ge(s_xt[0], 32)
        g.wait_ge(s_xt[1], 32)
        g.memset(A.bf("vst", P, NMC * VROW), 1.0)
        g.wait_ge(s_pe, pe0 + 2 * 16)
        g.dma_start(out=A.bf("w0", P, KC * D), in_=wvL[:, :]).then_inc(s_w[0], 16)

    rope_dst = {0: "qrope", 1: "krope"}

    with (
        nc.psum_tensor("ps_qk0", [P, QBW], F32) as ps_qk0,
        nc.psum_tensor("ps_qk1", [P, QBW], F32) as ps_qk1,
        nc.psum_tensor("ps_qs0", [P, QBW], F32) as ps_qs0,
        nc.psum_tensor("ps_qs1", [P, QBW], F32) as ps_qs1,
    ):
        ps_qk = [ps_qk0, ps_qk1]
        ps_qs = [ps_qs0, ps_qs1]

        def units():
            u = 0
            for p in range(2):
                for oc in range(KC):
                    for th in range(NQB):
                        yield u, p, oc, th
                        u += 1

        @blk.tensor
        def _(t):
            t.wait_ge(s_w[0], 16)
            t.wait_ge(s_w[1], 16)
            t.wait_ge(s_ve, ve0)
            def swap_mm(u):
                t.wait_ge(s_sc, sc0 + u + 1)
                if u >= 2:
                    t.wait_ge(s_ve, ve0 + (u - 2) + 1)
                for half in range(2):
                    ins = t.matmul(ps_qs[u % 2][:, half * 512:(half + 1) * 512],
                                   A.bf("swp", P, P),
                                   A.bf(f"qb{u % 2}", P, 512, col_off=half * 512),
                                   start=True, stop=True)
                ins.then_inc(s_pe, 1)

            for u, p, oc, th in units():
                if u >= 2:
                    t.wait_ge(s_sc, sc0 + (u - 2) + 1)
                wslot = f"w{p}"
                for kc in range(KC):
                    for half in range(2):
                        ins = t.matmul(
                            ps_qk[u % 2][:, half * 512:(half + 1) * 512],
                            A.bf(wslot, P, P, col_off=kc * D + oc * P),
                            A.bf("xn", P, 512,
                                 col_off=kc * T + th * QBW + half * 512),
                            start=(kc == 0), stop=(kc == KC - 1))
                ins.then_inc(s_pe, 1)
                if u >= 1:
                    swap_mm(u - 1)
            swap_mm(31)
            t.wait_ge(s_w[0], 32)
            t.wait_ge(s_sc, sc0 + 32)
            for m in range(NMC):
                if m >= 2:
                    t.wait_ge(s_sc, sc0 + 32 + (m - 2) + 1)
                for kc in range(KC):
                    for half in range(2):
                        t.matmul(ps_qk[m % 2][:, half * 512:(half + 1) * 512],
                                 A.bf("xn", P, P, col_off=kc * T + m * P),
                                 A.bf("w0", P, 512, col_off=kc * D + half * 512),
                                 start=(kc == 0), stop=False)
                for half in range(2):
                    ins = t.matmul(ps_qk[m % 2][:, half * 512:(half + 1) * 512],
                                   A.bf("onesrb", 1, P),
                                   A.bf("bvrow", 1, 512, col_off=half * 512),
                                   start=False, stop=True)
                ins.then_inc(s_pe, 1)
        C["pe"] = pe0 + 2 * 32 + 16

        @blk.scalar
        def _(sc):
            for u, p, oc, th in units():
                sc.wait_ge(s_pe, pe0 + (1 if u == 0 else 2 * u))
                if u >= 2:
                    sc.wait_ge(s_ve, ve0 + (u - 2) + 1)
                sc.activation(A.bf(f"qb{u % 2}", P, QBW), ps_qk[u % 2][:, :], AF.Identity,
                              bias=A.f32("bias4", P, 1, col_off=p * 8 + oc)).then_inc(s_sc, 1)
            for m in range(NMC):
                sc.wait_ge(s_pe, pe0 + 64 + m + 1)
                voff = A.live["vst"][0] // 2 + m * VROW
                dstap = bass.AP(A.hbf, voff, [[A.total // 2, P], [65, NH], [1, DH]])
                srcap = bass.AP(ps_qk[m % 2], 0, [[QBW, P], [DH, NH], [1, DH]])
                sc.activation(dstap, srcap, AF.Copy).then_inc(s_sc, 1)
        C["sc"] = sc0 + 32 + 16

        @blk.vector
        def _(v):
            for u, p, oc, th in units():
                v.wait_ge(s_sc, sc0 + u + 1)
                t1 = A.bf(f"t1s{u % 2}", P, QBW)
                t2 = A.bf(f"t2s{u % 2}", P, QBW)
                v.tensor_tensor(t1, A.bf(f"qb{u % 2}", P, QBW),
                                A.bf("tabs", P, QBW, col_off=th * QBW), ALU.mult)
                v.wait_ge(s_pe, pe0 + (2 * u + 3 if u < 31 else 64))
                v.tensor_tensor(t2, ps_qs[u % 2][:, :],
                                A.bf("tabs", P, QBW, col_off=T + th * QBW), ALU.mult)
                v.tensor_tensor(A.bf(rope_dst[p], P, QBW, col_off=oc * T + th * QBW),
                                t1, t2, ALU.add).then_inc(s_ve, 1)
        C["ve"] = ve0 + 32

    if PHASES < 3:
        @blk.sync
        def _(sy):
            sy.wait_ge(s_xnst, 16)
        return
    peA, scA, veA = C["pe"], C["sc"], C["ve"]
    A.free("w0"); A.free("w1")
    A.free("qb0"); A.free("qb1"); A.free("t1s0"); A.free("t1s1")
    A.free("t2s0"); A.free("t2s1")
    A.free("xn")
    A.free("tabs")
    A.free("swp")
    A.free("onesrf")
    A.free("bvrow")
    A.alloc("xnC0", KC * TTW * 2)
    A.alloc("xnC1", KC * TTW * 2)
    A.alloc("wis0", KC * P * 2)
    A.alloc("wis1", KC * P * 2)
    A.alloc("a01b", 2 * P * 2)
    A.alloc("b0p", FF * 2)
    A.alloc("b1b", FF * 2)
    A.alloc("xas0", TTW * 2)
    A.alloc("xas1", TTW * 2)
    A.alloc("xas0n", TTW * 2)
    A.alloc("h0s0", TTW * 2)
    A.alloc("h0s1", TTW * 2)
    A.alloc("h1s0", TTW * 2)
    A.alloc("h1s1", TTW * 2)
    A.alloc("hch0", TTW * 2)
    A.alloc("hch1", TTW * 2)
    A.alloc("hC0r", FC * TTW * 2)
    A.alloc("attn0", T * 2)
    A.alloc("attn1", T * 2)
    A.alloc("pt0", QBW * 2)
    A.alloc("pt1", QBW * 2)
    A.alloc("sumsb", QBW * 4)
    A.alloc("recf", 512 * 4)

    pei_pos = {}
    _k = 0
    for _tt in range(NTT):
        _k += 1; pei_pos[(_tt, "xA", 0)] = _k
        _k += 1; pei_pos[(_tt, "xA", 1)] = _k
        for _fc in range(FC):
            _k += 1; pei_pos[(_tt, "I", _fc)] = _k
            if _fc >= 1:
                _k += 1; pei_pos[(_tt, "D", _fc - 1)] = _k
        _k += 1; pei_pos[(_tt, "D", FC - 1)] = _k
    PEI_TOT = _k
    vei_pos = {}
    _k = 0
    for _tt in range(NTT):
        for _fc in range(FC):
            _k += 1; vei_pos[(_tt, "h0", _fc)] = _k
            if _fc >= 1:
                _k += 1; vei_pos[(_tt, "h1", _fc - 1)] = _k
                _k += 1; vei_pos[(_tt, "hadd", _fc - 1)] = _k
        _k += 1; vei_pos[(_tt, "h1", FC - 1)] = _k
        _k += 1; vei_pos[(_tt, "hadd", FC - 1)] = _k
    VEI_TOT = _k

    @blk.gpsimd
    def _(g):
        g.wait_ge(s_pe, peA)
        g.wait_ge(s_ve, veA)
        g.wait_ge(s_xnst, 16)
        g.memset(A.bf("xas0", P, TTW), 0.0)
        g.memset(A.bf("xas1", P, TTW), 0.0)
        g.memset(A.bf("xas0n", P, TTW), 0.0)
        g.memset(A.bf("b0p", P, FF), 0.0)
        g.memset(A.bf("b1b", P, FF), 0.0)
        g.dma_start(out=A.bf("a01b", P, 2 * P), in_=a01[:, :]).then_inc(s_cld, 16)
        g.dma_start(out=A.bf("b0p", RANK, FF), in_=b0T[:, :]).then_inc(s_cld, 16)
        g.dma_start(out=A.bf("b1b", RANK, FF), in_=b1T[:, :]).then_inc(s_cld, 16)
        for tt in range(2):
            srcx = bass.AP(xn_hbm, tt * TTW, [[KC * T, P], [T, KC], [1, TTW]])
            g.dma_start(out=A.bf(f"xnC{tt % 2}", P, KC * TTW),
                        in_=srcx).then_inc(s_xnin, 16)
        for idx in range(NTT * FC):
            tt, fc = divmod(idx, FC)
            if fc == 0 and tt >= 2:
                g.wait_ge(s_pei, pei_pos[(tt - 2, "D", FC - 1)])
                srcx = bass.AP(xn_hbm, tt * TTW, [[KC * T, P], [T, KC], [1, TTW]])
                g.dma_start(out=A.bf(f"xnC{tt % 2}", P, KC * TTW),
                            in_=srcx).then_inc(s_xnin, 16)
            if idx >= 2:
                tp, fp = divmod(idx - 2, FC)
                g.wait_ge(s_pei, pei_pos[(tp, "I", fp)])
            srcw = bass.AP(wiL, fc * P, [[KC * FF, P], [FF, KC], [1, P]])
            g.dma_start(out=A.bf(f"wis{idx % 2}", P, KC * P),
                        in_=srcw).then_inc(s_wis[idx % 2], 16)

    pe0, sc0, ve0 = peA, scA, veA
    QW = 512

    def pe_S_i(i):
        return pe0 + i + 1

    inter_pe = []

    with (
        nc.psum_tensor("ps_S0", [P, QBW], F32) as ps_S0,
        nc.psum_tensor("ps_S1", [P, QBW], F32) as ps_S1,
        nc.psum_tensor("ps_Oa", [65, QW], F32) as ps_Oa,
        nc.psum_tensor("ps_Ob", [65, QW], F32) as ps_Ob,
        nc.psum_tensor("ps_I0", [P, TTW], F32) as ps_I0,
        nc.psum_tensor("ps_I1", [P, TTW], F32) as ps_I1,
    ):
        ps_S = [ps_S0, ps_S1]
        ps_O = [ps_Oa, ps_Ob]
        ps_I = [ps_I0, ps_I1]

        def mk_xA(tt, e):
            def emit_half(hf):
                def emit(t):
                    if hf == 0:
                        if e == 0:
                            t.wait_ge(s_xnin, 16 * (tt + 1))
                        if tt >= 1:
                            t.wait_ge(s_vei, vei_pos[(tt - 1, "h1", FC - 2 + e)])
                    ins = None
                    for kc in range(hf * 4, hf * 4 + 4):
                        ins = t.matmul(ps_I[e][0:RANK, :],
                                       A.bf("a01b", P, RANK, col_off=e * P + kc * RANK),
                                       A.bf(f"xnC{tt % 2}", P, TTW, col_off=kc * TTW),
                                       start=(kc == 0), stop=(kc == KC - 1))
                    if hf == 1:
                        ins.then_inc(s_pei, 1)
                return emit
            inter_pe.append((4, (tt, "xA", e, 0), emit_half(0)))
            inter_pe.append((4, (tt, "xA", e), emit_half(1)))

        def mk_I(tt, fc):
            idx = tt * FC + fc
            def emit_half(hf):
                def emit(t):
                    if hf == 0:
                        t.wait_ge(s_wis[idx % 2], 16 * (idx // 2 + 1))
                        if fc == 0:
                            t.wait_ge(s_sci, tt * 3 + 1)
                        elif fc == 1:
                            t.wait_ge(s_sci, tt * 3 + 2)
                        if fc >= 2:
                            t.wait_ge(s_vei, vei_pos[(tt, "h1", fc - 2)])
                    for kc in range(hf * 4, hf * 4 + 4):
                        t.matmul(ps_I[fc % 2][:, :],
                                 A.bf(f"wis{idx % 2}", P, P, col_off=kc * P),
                                 A.bf(f"xnC{tt % 2}", P, TTW, col_off=kc * TTW),
                                 start=(kc == 0), stop=False)
                    if hf == 1:
                        t.matmul(ps_I[fc % 2][:, :],
                                 A.bf("b0p", P, P, col_off=fc * P),
                                 A.bf("xas0", P, TTW),
                                 start=False, stop=True).then_inc(s_pei, 1)
                return emit
            inter_pe.append((4, (tt, "I", fc, 0), emit_half(0)))
            inter_pe.append((5, (tt, "I", fc), emit_half(1)))

        def mk_D(tt, fc):
            def emit(t):
                t.wait_ge(s_vei, vei_pos[(tt, "h0", fc)])
                if fc == 0:
                    t.wait_ge(s_sci, tt * 3 + 3)
                t.matmul(ps_I[fc % 2][:, :],
                         A.bf("b1b", P, P, col_off=fc * P),
                         A.bf("xas1", P, TTW), start=False, stop=False,
                         skip_group_check=True)
                t.matmul(ps_I[fc % 2][:, :],
                         A.bf("b0p", P, P, col_off=fc * P),
                         A.bf("xas0n", P, TTW),
                         start=False, stop=True,
                         skip_group_check=True).then_inc(s_pei, 1)
            inter_pe.append((2, (tt, "D", fc), emit))

        for tt in range(NTT):
            mk_xA(tt, 0)
            mk_xA(tt, 1)
            for fc in range(FC):
                mk_I(tt, fc)
                if fc >= 1:
                    mk_D(tt, fc - 1)
            mk_D(tt, FC - 1)
        INTER_MM = sum(n for n, _, _ in inter_pe)
        step_of = {}

        @blk.tensor
        def _(t):
            t.wait_ge(s_ve, veA)
            t.wait_ge(s_sc, scA)
            t.wait_ge(s_cld, 48)

            def S_pair(i, inc):
                g, m = divmod(i, 16)
                c, qb4 = divmod(g, 4)
                ins = None
                for hh in range(2):
                    ins = t.matmul(
                        ps_S[i % 2][:, hh * QW:(hh + 1) * QW],
                        A.bf("krope", DH, P, row_off=hh * 64, col_off=c * T + m * P),
                        A.bf("qrope", DH, QW, row_off=hh * 64,
                             col_off=c * T + qb4 * QW),
                        start=True, stop=True)
                if inc:
                    ins.then_inc(s_pe2, 1)
                return ins

            for i in range(2):
                S_pair(i, True)
            iop = 0
            emitted = 0
            for i in range(512):
                g, m = divmod(i, 16)
                c, qb4 = divmod(g, 4)
                t.wait_ge(s_sc, sc0 + i + 1)
                if m == 0 and g >= 1:
                    t.wait_ge(s_ve, ve0 + 2 * g - 1)
                if i + 2 < 512:
                    S_pair(i + 2, True)
                for hh in range(2):
                    ins = t.matmul(
                        ps_O[hh][:, :],
                        A.bf("vst", P, 65, col_off=m * VROW + (2 * c + hh) * 65),
                        A.bf(f"pt{i % 2}", P, QW, col_off=hh * QW),
                        start=(m == 0), stop=(m == NMC - 1))
                ins.then_inc(s_pe, 1)
                target = (i + 1) * INTER_MM / 512.0
                while iop < len(inter_pe) and emitted < target:
                    n, key, fn = inter_pe[iop]
                    fn(t)
                    step_of[key] = i
                    emitted += n
                    iop += 1
            while iop < len(inter_pe):
                n, key, fn = inter_pe[iop]
                fn(t)
                step_of[key] = 511
                iop += 1
        C["pe"] = pe0 + 512

        def step_c(i):
            return divmod(divmod(i, 16)[0], 4)[0]

        sc_inserts = {}
        for tt in range(NTT):
            st = min(step_of[(tt, "xA", 1)] + 2, 511)
            def mk_xas(tt):
                def emit(sc):
                    sc.wait_ge(s_pei, pei_pos[(tt, "xA", 0)])
                    if tt >= 1:
                        sc.wait_ge(s_pei, pei_pos[(tt - 1, "I", FC - 1)])
                    sc.activation(A.bf("xas0", RANK, TTW), ps_I[0][0:RANK, :],
                                  AF.Copy).then_inc(s_sci, 1)
                    sc.wait_ge(s_pei, pei_pos[(tt, "xA", 1)])
                    if tt >= 1:
                        sc.wait_ge(s_pei, pei_pos[(tt - 1, "D", FC - 1)])
                    sc.activation(A.bf("xas1", RANK, TTW), ps_I[1][0:RANK, :],
                                  AF.Copy).then_inc(s_sci, 1)
                    sc.activation(A.bf("xas0n", RANK, TTW), A.bf("xas0", RANK, TTW),
                                  AF.Identity,
                                  scale=A.f32("cst", RANK, 1, col_off=4)).then_inc(s_sci, 1)
                return emit
            sc_inserts.setdefault(st, []).append(mk_xas(tt))

        @blk.scalar
        def _(sc):
            for i in range(512):
                sc.wait_ge(s_pe2, i + 1)
                if i >= 2:
                    sc.wait_ge(s_pe, pe0 + i - 1)
                sc.activation(A.bf(f"pt{i % 2}", P, QBW), ps_S[i % 2][:, :],
                              AF.Exp, bias=A.f32("cst", P, 1, col_off=1),
                              scale=A.f32("cst", P, 1, col_off=0)).then_inc(s_sc, 1)
                for fn in sc_inserts.get(i, []):
                    fn(sc)
        C["sc"] = sc0 + 512

        ve_inserts = {}

        def add_ve(st, emit):
            ve_inserts.setdefault(min(st, 511), []).append(emit)

        def mk_h0(tt, fc):
            def emit(v):
                v.wait_ge(s_pei, pei_pos[(tt, "I", fc)])
                v.tensor_scalar(A.bf(f"h0s{fc % 2}", P, TTW), ps_I[fc % 2][:, :],
                                A.f32("tkv", P, 1, col_off=0), 0.0,
                                ALU.mult, ALU.max).then_inc(s_vei, 1)
            return emit

        def mk_h1(tt, fc):
            def emit(v):
                v.wait_ge(s_pei, pei_pos[(tt, "D", fc)])
                v.tensor_scalar(A.bf(f"h1s{fc % 2}", P, TTW), ps_I[fc % 2][:, :],
                                A.f32("tkv", P, 1, col_off=1), 0.0,
                                ALU.mult, ALU.max).then_inc(s_vei, 1)
            return emit

        def mk_hadd(tt, fc):
            j = (tt - 1) * FC + fc
            def emit(v):
                if tt == 0:
                    dst = A.bf("hC0r", P, TTW, col_off=fc * TTW)
                else:
                    if j >= 2:
                        v.wait_ge(s_hst[j % 2], 16 * (j // 2))
                    dst = A.bf(f"hch{j % 2}", P, TTW)
                v.tensor_tensor(dst,
                                A.bf(f"h0s{fc % 2}", P, TTW),
                                A.bf(f"h1s{fc % 2}", P, TTW),
                                ALU.add).then_inc(s_vei, 1)
            return emit

        for tt in range(NTT):
            for fc in range(FC):
                add_ve(step_of[(tt, "I", fc)], mk_h0(tt, fc))
                if fc >= 1:
                    add_ve(step_of[(tt, "D", fc - 1)], mk_h1(tt, fc - 1))
                    add_ve(step_of[(tt, "D", fc - 1)], mk_hadd(tt, fc - 1))
            add_ve(step_of[(tt, "D", FC - 1)], mk_h1(tt, FC - 1))
            add_ve(step_of[(tt, "D", FC - 1)], mk_hadd(tt, FC - 1))

        @blk.vector
        def _(v):
            for i in range(512):
                g, m = divmod(i, 16)
                if m == 15:
                    c, qb4 = divmod(g, 4)
                    v.wait_ge(s_pe, pe0 + g * 16 + 16)
                    if c >= 2 and qb4 == 0:
                        v.wait_ge(s_ast[c % 2], 16 * (c // 2))
                    for hh in range(2):
                        v.tensor_copy(
                            A.bf(f"attn{c % 2}", 64, QW, row_off=hh * 64,
                                 col_off=qb4 * QW),
                            ps_O[hh][0:64, :])
                    ins = None
                    for hh in range(2):
                        ins = v.tensor_copy(A.f32("sumsb", 1, QW, col_off=hh * QW),
                                            ps_O[hh][64:65, :])
                    ins.then_inc(s_ve, 1)
                    if g >= 1:
                        v.wait_ge(s_rec, 16 * 2 * g)
                    for hh in range(2):
                        ins = v.tensor_copy(A.bf("recf", 1, QW, col_off=hh * QW),
                                            A.f32("sumsb", 1, QW, col_off=hh * QW))
                    ins.then_inc(s_ve, 1)
                for fn in ve_inserts.get(i, []):
                    fn(v)
        C["ve"] = ve0 + 2 * 32

        sy_inserts = {}
        for tt in range(1, NTT):
            for fc in range(FC):
                j = (tt - 1) * FC + fc
                def mk_hst(tt, fc, j):
                    def emit(sy):
                        sy.wait_ge(s_vei, vei_pos[(tt, "hadd", fc)])
                        sy.dma_start(
                            out=h_hbm[:, fc * T + tt * TTW: fc * T + tt * TTW + TTW],
                            in_=A.bf(f"hch{j % 2}", P, TTW)
                        ).then_inc(s_hst[j % 2], 16)
                    return emit
                st = min(step_of[(tt, "D", fc)] + 2, 511)
                sy_inserts.setdefault(st, []).append(mk_hst(tt, fc, j))

        @blk.sync
        def _(sy):
            for i in range(512):
                g, m = divmod(i, 16)
                if m == 15:
                    c, qb4 = divmod(g, 4)
                    sy.wait_ge(s_ve, ve0 + 2 * g + 2)
                    for hh in range(2):
                        sy.dma_start(
                            out=rec_hbm[2 * c + hh:2 * c + hh + 1,
                                        qb4 * QW:(qb4 + 1) * QW],
                            in_=A.bf("recf", 1, QW, col_off=hh * QW)
                        ).then_inc(s_rec, 16)
                    if qb4 == 3:
                        sy.wait_ge(s_ve, ve0 + 2 * g + 1)
                        sy.dma_start(out=env["at_hbm"][:, c * T:(c + 1) * T],
                                     in_=A.bf(f"attn{c % 2}", P, T)
                                     ).then_inc(s_ast[c % 2], 16)
                for fn in sy_inserts.get(i, []):
                    fn(sy)

    if PHASES < 4:
        @blk.sync
        def _(sy):
            sy.wait_ge(s_ast[0], 64)
            sy.wait_ge(s_ast[1], 64)
            sy.wait_ge(s_hst[0], 16 * 16 * (NTT - 1))
            sy.wait_ge(s_hst[1], 16 * 16 * (NTT - 1))
        return
    peB2, scB2, veB2 = C["pe"], C["sc"], C["ve"]
    A.free("qrope"); A.free("krope"); A.free("vst")
    A.free("pt0"); A.free("pt1")
    A.free("sumsb"); A.free("recf")
    A.free("attn0"); A.free("attn1")
    A.free("xnC0"); A.free("xnC1"); A.free("wis0"); A.free("wis1")
    A.free("a01b"); A.free("b0p"); A.free("b1b")
    A.free("xas0"); A.free("xas1"); A.free("xas0n")
    A.free("h0s0"); A.free("h0s1"); A.free("h1s0"); A.free("h1s1")
    A.free("hch0"); A.free("hch1")
    A.alloc("rec_all", T * 2)
    A.alloc("recallf", T * 4)
    A.alloc("woP", KC * D * 2)
    A.alloc("hC0", FC * TTW * 2)
    A.alloc("hC1", FC * TTW * 2)
    A.alloc("wos0", FF * 2)
    A.alloc("wos1", FF * 2)
    A.alloc("wos2", FF * 2)
    A.alloc("atC0", KC * TTW * 2)
    A.alloc("atC1", KC * TTW * 2)
    A.alloc("xcc0", TTW * 4)
    A.alloc("xcc1", TTW * 4)
    A.alloc("outc0", TTW * 4)
    A.alloc("outc1", TTW * 4)
    A.alloc("t2s0c", TTW * 4)
    A.alloc("t2s1c", TTW * 4)

    pe_pos2 = {}
    _k = 0
    for _j in range(KC):
        _k += 1; pe_pos2[(0, "rb", _j)] = _k
    for _tt in range(NTT):
        for _j in range(KC):
            _k += 1; pe_pos2[(_tt, "F", _j)] = _k
            if _tt + 1 < NTT:
                _k += 1; pe_pos2[(_tt + 1, "rb", _j)] = _k
    PE2_TOT = _k
    ve_pos2 = {}
    _k = 2
    ve_pos2[("rec", 0)] = 2
    for _j in range(KC):
        _k += 1; ve_pos2[(0, "rbTT", _j)] = _k
    for _tt in range(NTT):
        if _tt + 1 < NTT:
            _k += 2; ve_pos2[("rec", _tt + 1)] = _k
        for _j in range(KC):
            _k += 1; ve_pos2[(_tt, "out", _j)] = _k
            if _tt + 1 < NTT:
                _k += 1; ve_pos2[(_tt + 1, "rbTT", _j)] = _k
    VE2_TOT = _k

    def peC2(tt, key, j):
        return peB2 + pe_pos2[(tt, key, j)]

    def scC2(tt, oc):
        return scB2 + tt * KC + oc + 1

    def veC2(tt, key, j):
        return veB2 + ve_pos2[(tt, key, j)]

    @blk.gpsimd
    def _(g):
        g.wait_ge(s_pe, peB2)
        g.wait_ge(s_vei, VEI_TOT)
        g.wait_ge(s_hst[0], 16 * 16 * (NTT - 1))
        g.wait_ge(s_hst[1], 16 * 16 * (NTT - 1))
        g.wait_ge(s_ast[0], 64)
        g.wait_ge(s_ast[1], 64)
        g.wait_ge(s_rec, 16 * 64)
        g.dma_start(out=A.bf("rec_all", RANK, T), in_=rec_hbm[:, :]).then_inc(s_cld, 16)
        g.dma_start(out=A.bf("woP", P, KC * D), in_=woPL[:, :]).then_inc(s_wop, 16)
        for tt in range(NTT):
            for oc in range(KC):
                idx = tt * KC + oc
                if idx >= 3:
                    t3p, o3p = divmod(idx - 3, KC)
                    g.wait_ge(s_pe, peC2(t3p, "F", o3p))
                for hf in range(2):
                    g.dma_start(out=A.bf(f"wos{idx % 3}", P, FF // 2, col_off=hf * FF // 2),
                                in_=woL[:, oc * FF + hf * FF // 2:
                                        oc * FF + (hf + 1) * FF // 2]).then_inc(s_wos, 16)

    @blk.sync
    def _(sy):
        sy.wait_ge(s_pe, peB2)
        sy.wait_ge(s_vei, VEI_TOT)
        sy.wait_ge(s_hst[0], 16 * 16 * (NTT - 1))
        sy.wait_ge(s_hst[1], 16 * 16 * (NTT - 1))
        sy.wait_ge(s_ast[0], 64)
        sy.wait_ge(s_ast[1], 64)
        sy.wait_ge(s_rec, 16 * 64)

        def loads(tt):
            if tt >= 2:
                sy.wait_ge(s_pe, peC2(tt - 2, "F", KC - 1))
            src = bass.AP(at_hbm, tt * TTW, [[KC * T, P], [T, KC], [1, TTW]])
            sy.dma_start(out=A.bf(f"atC{tt % 2}", P, KC * TTW),
                         in_=src).then_inc(s_att[tt % 2], 16)

        def loads_h(tt):
            if tt >= 3:
                sy.wait_ge(s_pe, peC2(tt - 2, "F", KC - 1))
            srch = bass.AP(h_hbm, tt * TTW, [[FC * T, P], [T, FC], [1, TTW]])
            sy.dma_start(out=A.bf(f"hC{tt % 2}", P, FC * TTW),
                         in_=srch).then_inc(s_hld, 16)

        def xcc_load(idx2):
            tt2, oc2 = divmod(idx2, KC)
            sy.dma_start(out=A.f32(f"xcc{idx2 % 2}", P, TTW),
                         in_=xR[:, oc2 * T + tt2 * TTW: oc2 * T + tt2 * TTW + TTW]
                         ).then_inc(s_xc[idx2 % 2], 16)

        def stores(tt):
            for oc in range(KC):
                idx = tt * KC + oc
                sy.wait_ge(s_ve, veC2(tt, "out", oc))
                sy.dma_start(out=yT[oc * P:(oc + 1) * P, tt * TTW:(tt + 1) * TTW],
                             in_=A.f32(f"outc{idx % 2}", P, TTW)).then_inc(s_yst[idx % 2], 16)
                if idx + 2 < NTT * KC:
                    xcc_load(idx + 2)

        xcc_load(0)
        xcc_load(1)
        loads(0)
        loads(1)
        loads_h(1)
        loads_h(2)
        for tt in range(1, NTT):
            stores(tt - 1)
            if tt + 1 < NTT:
                loads(tt + 1)
            if tt == 2:
                loads_h(3)
        stores(NTT - 1)
        sy.wait_ge(s_yst[0], 16 * 16)
        sy.wait_ge(s_yst[1], 16 * 16)

    with (
        nc.psum_tensor("ps_F0", [P, TTW], F32) as ps_F0,
        nc.psum_tensor("ps_F1", [P, TTW], F32) as ps_F1,
        nc.psum_tensor("ps_rb0", [P, TTW], F32) as ps_rb0,
        nc.psum_tensor("ps_rb1", [P, TTW], F32) as ps_rb1,
    ):
        ps_F = [ps_F0, ps_F1]
        ps_rb = [ps_rb0, ps_rb1]

        @blk.tensor
        def _(t):
            t.wait_ge(s_wop, 16)

            def rb_mm(tt, fc):
                if fc == 0:
                    t.wait_ge(s_ve, veB2 + ve_pos2[("rec", tt)])
                if fc >= 2:
                    t.wait_ge(s_ve, veC2(tt, "rbTT", fc - 2))
                elif tt >= 1:
                    t.wait_ge(s_ve, veC2(tt - 1, "rbTT", KC - 2 + fc))
                t.matmul(ps_rb[fc % 2][:, :], A.bf("sel16", P, P, col_off=fc * P),
                         A.bf("rec_all", P, TTW, col_off=tt * TTW),
                         start=True, stop=True).then_inc(s_pe, 1)

            for fc in range(KC):
                rb_mm(0, fc)
            for tt in range(NTT):
                t.wait_ge(s_ve, veC2(tt, "rbTT", KC - 1))
                if tt == 0:
                    t.wait_ge(s_vei, vei_pos[(0, "hadd", FC - 1)])
                else:
                    t.wait_ge(s_hld, 16 * tt)
                for oc in range(KC):
                    idx = tt * KC + oc
                    if oc >= 2:
                        t.wait_ge(s_sc, scC2(tt, oc - 2))
                    elif tt >= 1:
                        t.wait_ge(s_sc, scC2(tt - 1, KC - 2 + oc))
                    t.wait_ge(s_wos, 32 * (idx + 1))
                    for kc in range(KC):
                        t.matmul(ps_F[oc % 2][:, :],
                                 A.bf("woP", P, P, col_off=kc * D + oc * P),
                                 A.bf(f"atC{tt % 2}", P, TTW, col_off=kc * TTW),
                                 start=(kc == 0), stop=False)
                    hsrc = "hC0r" if tt == 0 else f"hC{tt % 2}"
                    for fc in range(FC):
                        ins = t.matmul(ps_F[oc % 2][:, :],
                                       A.bf(f"wos{idx % 3}", P, P, col_off=fc * P),
                                       A.bf(hsrc, P, TTW, col_off=fc * TTW),
                                       start=False, stop=(fc == FC - 1))
                    ins.then_inc(s_pe, 1)
                    if tt + 1 < NTT:
                        rb_mm(tt + 1, oc)
        C["pe"] = peB2 + PE2_TOT

        @blk.scalar
        def _(sc):
            for tt in range(NTT):
                for oc in range(KC):
                    idx = tt * KC + oc
                    sc.wait_ge(s_pe, peC2(tt, "F", oc))
                    if idx >= 2:
                        t2p, o2p = divmod(idx - 2, KC)
                        sc.wait_ge(s_ve, veC2(t2p, "out", o2p))
                    sc.activation(A.f32(f"t2s{oc % 2}c", P, TTW), ps_F[oc % 2][:, :],
                                  AF.Identity,
                                  bias=A.f32("bias4", P, 1, col_off=24 + oc)).then_inc(s_sc, 1)
        C["sc"] = scB2 + NTT * KC

        @blk.vector
        def _(v):
            def rec_slice(tt):
                v.reciprocal(A.f32("recallf", RANK, TTW, col_off=tt * TTW),
                             A.bf("rec_all", RANK, TTW, col_off=tt * TTW)).then_inc(s_ve, 1)
                v.tensor_copy(A.bf("rec_all", RANK, TTW, col_off=tt * TTW),
                              A.f32("recallf", RANK, TTW, col_off=tt * TTW)).then_inc(s_ve, 1)

            def rbTT(tt, fc):
                v.wait_ge(s_pe, peC2(tt, "rb", fc))
                if fc == 0:
                    v.wait_ge(s_att[tt % 2], 16 * (tt // 2 + 1))
                v.tensor_tensor(A.bf(f"atC{tt % 2}", P, TTW, col_off=fc * TTW),
                                A.bf(f"atC{tt % 2}", P, TTW, col_off=fc * TTW),
                                ps_rb[fc % 2][:, :], ALU.mult).then_inc(s_ve, 1)

            v.wait_ge(s_cld, 64)
            rec_slice(0)
            for fc in range(KC):
                rbTT(0, fc)
            for tt in range(NTT):
                if tt + 1 < NTT:
                    rec_slice(tt + 1)
                for oc in range(KC):
                    idx = tt * KC + oc
                    v.wait_ge(s_sc, scC2(tt, oc))
                    v.wait_ge(s_xc[idx % 2], 16 * (idx // 2 + 1))
                    if idx >= 2:
                        v.wait_ge(s_yst[idx % 2], 16 * (idx // 2))
                    v.tensor_tensor(A.f32(f"outc{idx % 2}", P, TTW),
                                    A.f32(f"t2s{oc % 2}c", P, TTW),
                                    A.f32(f"xcc{idx % 2}", P, TTW), ALU.add).then_inc(s_ve, 1)
                    if tt + 1 < NTT:
                        rbTT(tt + 1, oc)
        C["ve"] = veB2 + VE2_TOT



def _rope_perm():
    perm = []
    for h in range(NH):
        perm += [h * DH + 2 * i for i in range(DH // 2)]
        perm += [h * DH + 2 * i + 1 for i in range(DH // 2)]
    return np.array(perm, dtype=np.int64)


def _chunk_tile(wT):
    n = wT.shape[1]
    return np.ascontiguousarray(wT.reshape(KC, P, n).transpose(1, 0, 2).reshape(P, KC * n))


def _tables():
    inv = 1.0 / (10000.0 ** (np.arange(0, DH, 2, dtype=np.float64) / DH))
    pos = np.arange(T, dtype=np.float64)
    th = pos[None, :] * inv[:, None]
    cos = np.cos(th).astype(np.float32)
    sin = np.sin(th).astype(np.float32)
    ccc = np.concatenate([cos, cos, cos, cos], axis=0)
    ssp = np.concatenate([-sin, sin, -sin, sin], axis=0)
    return np.ascontiguousarray(np.concatenate([ccc, ssp], axis=1))


def _sel16():
    s = np.zeros((RANK, KC * P), np.float32)
    for c in range(KC):
        for p in range(P):
            s[2 * c + (p >= 64), c * P + p] = 1.0
    return s


def _swap_mat():
    s = np.zeros((P, P), np.float32)
    for g in range(2):
        for i in range(32):
            s[g * 64 + i, g * 64 + 32 + i] = 1.0
            s[g * 64 + 32 + i, g * 64 + i] = 1.0
    return s


def make_common_inputs(Wq, bq, Wk, bk, Wv, bv, Wo, bo, wi_w, wo_w,
                       lora_As, lora_Bs, tki, tkvals):
    perm = _rope_perm()
    wq_p = Wq[perm]
    wk_p = Wk[perm]
    bq_p = bq[perm]
    bk_p = bk[perm]
    e0 = int(tki[0]); e1 = int(tki[1])
    d = {
        "wqL": _chunk_tile(np.ascontiguousarray(wq_p.T)),
        "wkL": _chunk_tile(np.ascontiguousarray(wk_p.T)),
        "wvL": _chunk_tile(np.ascontiguousarray(Wv.T)),
        "woPL": _chunk_tile(np.ascontiguousarray(Wo.T)),
        "wiL": _chunk_tile(np.ascontiguousarray(wi_w.T)),
        "woL": np.ascontiguousarray(
            wo_w.T.reshape(FC, P, KC, P).transpose(1, 2, 0, 3).reshape(P, KC * FF)),
        "a01": np.ascontiguousarray(
            np.stack([lora_As[e0].T, lora_As[e1].T], axis=0)
            .reshape(2, KC, P, RANK).transpose(2, 0, 1, 3).reshape(P, 2 * P)),
        "b0T": np.ascontiguousarray(lora_Bs[e0].T),
        "b1T": np.ascontiguousarray(lora_Bs[e1].T),
        "bias4": np.ascontiguousarray(
            np.stack([bq_p.reshape(KC, P), bk_p.reshape(KC, P),
                      bv.reshape(KC, P), bo.reshape(KC, P)], axis=0)
            .reshape(4 * KC, P).T),
        "bvrow": np.ascontiguousarray(bv[None, :]),
        "tkv": np.ascontiguousarray(np.broadcast_to(tkvals[None, :], (P, 2))),
        "tabs": _tables(),
        "swp": _swap_mat(),
        "onesrf": np.ones((1, P), np.float32),
        "sel16": _sel16(),
    }
    return {k: np.ascontiguousarray(v.astype(np.float32)) for k, v in d.items()}


def kernel(L0_attn_x, L0_attn_Wq, L0_attn_bq, L0_attn_Wk, L0_attn_bk,
           L0_attn_Wv, L0_attn_bv, L0_attn_Wo, L0_attn_bo,
           L0_ffn_wi_w, L0_ffn_wo_w, L0_ffn_lora_As, L0_ffn_lora_Bs,
           L0_ffn_top_k_indices, L0_ffn_top_k_values):
    x = np.asarray(L0_attn_x, dtype=np.float32)
    args = [np.asarray(a, dtype=np.float32) for a in
            (L0_attn_Wq, L0_attn_bq, L0_attn_Wk, L0_attn_bk, L0_attn_Wv,
             L0_attn_bv, L0_attn_Wo, L0_attn_bo, L0_ffn_wi_w, L0_ffn_wo_w,
             L0_ffn_lora_As, L0_ffn_lora_Bs)]
    tki = np.asarray(L0_ffn_top_k_indices)
    tkvals = np.asarray(L0_ffn_top_k_values, dtype=np.float32)
    (Wq, bq, Wk, bk, Wv, bv, Wo, bo, wi_w, wo_w, lora_As, lora_Bs) = args

    nc = build_core_program()
    common = make_common_inputs(Wq, bq, Wk, bk, Wv, bv, Wo, bo,
                                wi_w, wo_w, lora_As, lora_Bs, tki, tkvals)
    in_maps = []
    for c in range(8):
        m = dict(common)
        m["xR"] = _chunk_tile(np.ascontiguousarray(x[c].T))
        in_maps.append(m)
    res = run_bass_kernel_spmd(nc, in_maps, core_ids=list(range(8)))
    out = np.empty((8, T, D), np.float32)
    for c in range(8):
        out[c] = res.results[c]["yT"].T
    return out


# version 29 (final)
# speedup vs baseline: 1.0170x; 1.0170x over previous
import numpy as np
import ml_dtypes
import concourse.bass as bass
import concourse.mybir as mybir
from concourse.bass_utils import run_bass_kernel_spmd

F32 = mybir.dt.float32
BF16 = mybir.dt.bfloat16
U8 = mybir.dt.uint8
AF = mybir.ActivationFunctionType
ALU = mybir.AluOpType

P = 128
T = 2048
D = 1024
KC = 8
NH = 16
DH = 64
FF = 4096
FC = 32
NTT = 4
TTW = 512
NQB = 2
QBW = 1024
NMC = 16
VROW = NH * 65
RANK = 16

EXP_SCALE = 0.125
EXP_SHIFT = -4.0
LN_EPS = 1e-5
PHASES = 4


class Arena:

    def __init__(self, nc, total_bytes):
        self.total = total_bytes
        self.h8 = nc.alloc_sbuf_tensor("arena", [P, total_bytes], U8)
        self.hbf = self.h8.bitcast(BF16)
        self.hf32 = self.h8.bitcast(F32)
        self.free_list = [(0, total_bytes)]
        self.live = {}

    def alloc(self, name, nbytes):
        nbytes = (nbytes + 127) & ~127
        best = None
        for i, (off, sz) in enumerate(self.free_list):
            if sz >= nbytes and (best is None or sz < self.free_list[best][1]):
                best = i
        if best is None:
            raise MemoryError(f"arena: cannot alloc {name} ({nbytes}B); free={self.free_list}")
        off, sz = self.free_list[best]
        self.free_list[best] = (off + nbytes, sz - nbytes)
        if self.free_list[best][1] == 0:
            del self.free_list[best]
        self.live[name] = (off, nbytes)
        return off

    def free(self, name):
        off, sz = self.live.pop(name)
        self.free_list.append((off, sz))
        self.free_list.sort()
        merged = []
        for o, s in self.free_list:
            if merged and merged[-1][0] + merged[-1][1] == o:
                merged[-1] = (merged[-1][0], merged[-1][1] + s)
            else:
                merged.append((o, s))
        self.free_list = [tuple(m) for m in merged]

    def bf(self, name, p, n, row_off=0, col_off=0):
        off = self.live[name][0]
        assert off % 2 == 0
        return self.hbf[row_off:row_off + p, off // 2 + col_off: off // 2 + col_off + n]

    def f32(self, name, p, n, row_off=0, col_off=0):
        off = self.live[name][0]
        assert off % 4 == 0
        return self.hf32[row_off:row_off + p, off // 4 + col_off: off // 4 + col_off + n]


def build_core_program():
    nc = bass.Bass("TRN2", target_bir_lowering=False, detect_race_conditions=False)

    xR = nc.dram_tensor("xR", [P, KC * T], F32, kind="ExternalInput")
    wqL = nc.dram_tensor("wqL", [P, KC * D], F32, kind="ExternalInput")
    wkL = nc.dram_tensor("wkL", [P, KC * D], F32, kind="ExternalInput")
    wvL = nc.dram_tensor("wvL", [P, KC * D], F32, kind="ExternalInput")
    woPL = nc.dram_tensor("woPL", [P, KC * D], F32, kind="ExternalInput")
    wiL = nc.dram_tensor("wiL", [P, KC * FF], F32, kind="ExternalInput")
    woL = nc.dram_tensor("woL", [P, KC * FF], F32, kind="ExternalInput")
    a01 = nc.dram_tensor("a01", [P, 2 * 128], F32, kind="ExternalInput")
    b0T = nc.dram_tensor("b0T", [RANK, FF], F32, kind="ExternalInput")
    b1T = nc.dram_tensor("b1T", [RANK, FF], F32, kind="ExternalInput")
    bias4 = nc.dram_tensor("bias4", [P, 32], F32, kind="ExternalInput")
    bvrow = nc.dram_tensor("bvrow", [1, D], F32, kind="ExternalInput")
    tkv = nc.dram_tensor("tkv", [P, 2], F32, kind="ExternalInput")
    tabs = nc.dram_tensor("tabs", [P, 2 * T], F32, kind="ExternalInput")
    swp = nc.dram_tensor("swp", [P, P], F32, kind="ExternalInput")
    onesrf = nc.dram_tensor("onesrf", [1, P], F32, kind="ExternalInput")
    sel16 = nc.dram_tensor("sel16", [RANK, KC * P], F32, kind="ExternalInput")
    yT = nc.dram_tensor("yT", [D, T], F32, kind="ExternalOutput")
    xn_hbm = nc.dram_tensor("xn_hbm", [P, KC * T], BF16, kind="Internal")
    at_hbm = nc.dram_tensor("at_hbm", [P, KC * T], BF16, kind="Internal")
    rec_hbm = nc.dram_tensor("rec_hbm", [RANK, T], BF16, kind="Internal")
    h_hbm = nc.dram_tensor("h_hbm", [P, FC * T], BF16, kind="Internal")

    A = Arena(nc, (nc.sbuf_bytes_remaining - 1024) & ~127)

    A.alloc("bias4", 32 * 4)
    A.alloc("tkv", 2 * 4)
    A.alloc("cst", 8 * 4)
    A.alloc("swp", P * 2)
    A.alloc("onesrf", P * 4)
    A.alloc("onesrb", P * 2)
    A.alloc("onescb", 128)
    A.alloc("onescf", 128)
    A.alloc("bvrow", D * 2)
    A.alloc("tabs", 2 * T * 2)
    A.alloc("sel16", KC * P * 2)
    A.alloc("xn", KC * T * 2)

    from contextlib import ExitStack
    with ExitStack() as stk:
        names = ["s_cst", "s_w0", "s_w1", "s_wi", "s_wop", "s_wo0", "s_wo1",
                 "s_cld", "s_xt0", "s_xt1", "s_xnin", "s_att0", "s_att1",
                 "s_xc0", "s_xc1", "s_xnst", "s_ast0", "s_ast1", "s_yst0",
                 "s_yst1", "s_pe", "s_sc", "s_ve", "s_rec", "s_sq",
                 "s_pei", "s_sci", "s_vei", "s_wis0", "s_wis1",
                 "s_hst0", "s_hst1", "s_hld", "s_pe2", "s_wos"]
        S = {n: stk.enter_context(nc.semaphore(n)) for n in names}
        blk = stk.enter_context(nc.Block())

        env = dict(
            xR=xR, wqL=wqL, wkL=wkL, wvL=wvL, woPL=woPL, wiL=wiL, woL=woL,
            a01=a01, b0T=b0T, b1T=b1T, bias4=bias4, bvrow=bvrow, tkv=tkv,
            tabs=tabs, swp=swp, onesrf=onesrf, sel16=sel16, yT=yT, xn_hbm=xn_hbm,
            at_hbm=at_hbm, rec_hbm=rec_hbm,
            s_cst=S["s_cst"], s_wi=S["s_wi"], s_wop=S["s_wop"],
            s_cld=S["s_cld"], s_xnin=S["s_xnin"], s_xnst=S["s_xnst"],
            s_pe=S["s_pe"], s_sc=S["s_sc"], s_ve=S["s_ve"],
            s_xt=[S["s_xt0"], S["s_xt1"]], s_w=[S["s_w0"], S["s_w1"]],
            s_wo=[S["s_wo0"], S["s_wo1"]], s_att=[S["s_att0"], S["s_att1"]],
            s_xc=[S["s_xc0"], S["s_xc1"]], s_ast=[S["s_ast0"], S["s_ast1"]],
            s_yst=[S["s_yst0"], S["s_yst1"]], s_rec=S["s_rec"], s_sq=S["s_sq"],
            s_pei=S["s_pei"], s_sci=S["s_sci"], s_vei=S["s_vei"],
            s_wis=[S["s_wis0"], S["s_wis1"]],
            s_hst=[S["s_hst0"], S["s_hst1"]], s_hld=S["s_hld"],
            s_pe2=S["s_pe2"], s_wos=S["s_wos"],
            h_hbm=h_hbm,
        )
        C = dict(pe=0, sc=0, ve=0)
        with nc.allow_low_precision(reason="bf16 working precision by design"):
            _build_body(nc, blk, A, C, env)
    return nc


def _build_body(nc, blk, A, C, env):
    xR = env["xR"]; wqL = env["wqL"]; wkL = env["wkL"]; wvL = env["wvL"]
    woPL = env["woPL"]; wiL = env["wiL"]; woL = env["woL"]
    a01 = env["a01"]; b0T = env["b0T"]; b1T = env["b1T"]
    bias4 = env["bias4"]; bvrow = env["bvrow"]; tkv = env["tkv"]
    tabs = env["tabs"]; swp = env["swp"]; onesrf = env["onesrf"]; sel16 = env["sel16"]
    yT = env["yT"]; xn_hbm = env["xn_hbm"]; at_hbm = env["at_hbm"]; rec_hbm = env["rec_hbm"]
    s_cst = env["s_cst"]; s_w = env["s_w"]; s_wi = env["s_wi"]; s_wop = env["s_wop"]
    s_wo = env["s_wo"]; s_cld = env["s_cld"]; s_xt = env["s_xt"]; s_xnin = env["s_xnin"]
    s_att = env["s_att"]; s_xc = env["s_xc"]; s_xnst = env["s_xnst"]
    s_ast = env["s_ast"]; s_yst = env["s_yst"]
    s_pe = env["s_pe"]; s_sc = env["s_sc"]; s_ve = env["s_ve"]; s_rec = env["s_rec"]
    s_sq = env["s_sq"]
    s_pei = env["s_pei"]; s_sci = env["s_sci"]; s_vei = env["s_vei"]
    s_wis = env["s_wis"]; s_hst = env["s_hst"]; s_hld = env["s_hld"]
    h_hbm = env["h_hbm"]; s_pe2 = env["s_pe2"]; s_wos = env["s_wos"]

    A.alloc("xt0", TTW * KC * 4)
    A.alloc("xt1", TTW * KC * 4)
    A.alloc("xt2", TTW * KC * 4)
    A.alloc("xsq0", TTW * KC * 2)
    A.alloc("xsq1", TTW * KC * 2)
    A.alloc("tmpA0", TTW * 4)
    A.alloc("tmpA1", TTW * 4)
    A.alloc("mrow0", TTW * 4, )
    A.alloc("mrow1", TTW * 4)
    A.alloc("rrow0", TTW * 4)
    A.alloc("rrow1", TTW * 4)
    A.alloc("exrow0", TTW * 4)
    A.alloc("exrow1", TTW * 4)
    A.alloc("var0", TTW * 4)
    A.alloc("var1", TTW * 4)
    A.alloc("w0", KC * D * 2)
    A.alloc("w1", KC * D * 2)
    A.alloc("qb0", QBW * 2)
    A.alloc("qb1", QBW * 2)
    A.alloc("t1s0", QBW * 2)
    A.alloc("t1s1", QBW * 2)
    A.alloc("t2s0", QBW * 2)
    A.alloc("t2s1", QBW * 2)

    def xt(i):
        return A.f32(f"xt{i % 3}", P, KC * TTW)

    @blk.gpsimd
    def _(g):
        g.memset(A.f32("cst", P, 1, col_off=0), EXP_SCALE)
        g.memset(A.f32("cst", P, 1, col_off=1), EXP_SHIFT)
        g.memset(A.f32("cst", P, 1, col_off=2), LN_EPS)
        g.memset(A.f32("cst", P, 1, col_off=3), -0.5)
        g.memset(A.f32("cst", P, 1, col_off=4), -1.0)
        g.memset(A.bf("onesrb", 1, P), 1.0)
        g.memset(A.f32("onescf", P, 1), 1.0)

        g.memset(A.bf("onescb", P, 1), 1.0)
        g.dma_start(out=A.f32("bias4", P, 32), in_=bias4[:, :]).then_inc(s_cst, 16)
        g.dma_start(out=A.f32("tkv", P, 2), in_=tkv[:, :]).then_inc(s_cst, 16)
        g.dma_start(out=A.bf("swp", P, P), in_=swp[:, :]).then_inc(s_cst, 16)
        g.dma_start(out=A.f32("onesrf", 1, P), in_=onesrf[:, :]).then_inc(s_cst, 16)
        g.memset(A.bf("sel16", P, KC * P), 0.0)
        g.dma_start(out=A.bf("sel16", RANK, KC * P), in_=sel16[:, :]).then_inc(s_cst, 16)
        g.dma_start(out=A.bf("bvrow", 1, D), in_=bvrow[:, :]).then_inc(s_cst, 16)
        g.dma_start(out=A.bf("tabs", P, 2 * T), in_=tabs[:, :]).then_inc(s_cst, 16)
        g.dma_start(out=A.bf("w0", P, KC * D), in_=wqL[:, :]).then_inc(s_w[0], 16)
        g.dma_start(out=A.bf("w1", P, KC * D), in_=wkL[:, :]).then_inc(s_w[1], 16)

    pe0, sc0, ve0 = C["pe"], C["sc"], C["ve"]

    @blk.sync
    def _(sy):
        for tt in range(NTT):
            if tt >= 3:
                sy.wait_ge(s_ve, ve0 + [3, 5, 7, 8][tt - 3])
            src = bass.AP(xR, tt * TTW, [[KC * T, P], [T, KC], [1, TTW]])
            sy.dma_start(out=xt(tt), in_=src).then_inc(s_xt[tt % 2], 16)

    @blk.gpsimd
    def _(g):
        for tt in range(NTT):
            g.wait_ge(s_xt[tt % 2], 16 * (tt // 2 + 1))
            if tt >= 2:
                g.wait_ge(s_pe, pe0 + 2 * (tt - 2) + 1)
            for hf in range(2):
                g.tensor_tensor(A.bf(f"xsq{tt % 2}", P, 4 * TTW, col_off=hf * 4 * TTW),
                                A.f32(f"xt{tt % 3}", P, 4 * TTW, col_off=hf * 4 * TTW),
                                A.f32(f"xt{tt % 3}", P, 4 * TTW, col_off=hf * 4 * TTW),
                                ALU.mult).then_inc(s_sq, 1)

    def sc_rstd(tt):
        return sc0 + tt + 1

    @blk.scalar
    def _(sc):
        for tt in range(NTT):
            sc.wait_ge(s_ve, ve0 + [1, 2, 4, 6][tt])
            var = A.f32(f"var{tt % 2}", 1, TTW)
            sc.activation(var, var, AF.Ln, bias=A.f32("cst", 1, 1, col_off=2))
            sc.activation(A.f32(f"rrow{tt % 2}", 1, TTW), var, AF.Exp,
                          scale=A.f32("cst", 1, 1, col_off=3)).then_inc(s_sc, 1)
    C["sc"] = sc0 + 4

    with (
        nc.psum_tensor("ps_sum0", [1, TTW], F32) as ps_sum0,
        nc.psum_tensor("ps_sum1", [1, TTW], F32) as ps_sum1,
        nc.psum_tensor("ps_sq0", [1, TTW], F32) as ps_sq0,
        nc.psum_tensor("ps_sq1", [1, TTW], F32) as ps_sq1,
        nc.psum_tensor("ps_m0", [P, TTW], F32) as ps_m0,
        nc.psum_tensor("ps_m1", [P, TTW], F32) as ps_m1,
        nc.psum_tensor("ps_r0", [P, TTW], F32) as ps_r0,
        nc.psum_tensor("ps_r1", [P, TTW], F32) as ps_r1,
    ):
        ps_sum = [ps_sum0, ps_sum1]
        ps_sq = [ps_sq0, ps_sq1]
        ps_m = [ps_m0, ps_m1]
        ps_r = [ps_r0, ps_r1]

        @blk.tensor
        def _(t):
            t.wait_ge(s_cst, 16 * 7)
            for tt in range(NTT):
                if tt >= 2:
                    t.wait_ge(s_ve, ve0 + [1, 2, 4, 6][tt - 2])
                onesc = A.bf("onescb", P, 1)
                t.wait_ge(s_xt[tt % 2], 16 * (tt // 2 + 1))
                for kc in range(KC):
                    t.matmul(ps_sum[tt % 2][0:1, :], A.f32("onescf", P, 1),
                             A.f32(f"xt{tt % 3}", P, TTW, col_off=kc * TTW),
                             start=(kc == 0), stop=(kc == KC - 1))
                for kc in range(KC):
                    if kc % 4 == 0:
                        t.wait_ge(s_sq, 2 * tt + kc // 4 + 1)
                    ins = t.matmul(ps_sq[tt % 2][0:1, :], onesc,
                                   A.bf(f"xsq{tt % 2}", P, TTW, col_off=kc * TTW),
                                   start=(kc == 0), stop=(kc == KC - 1))
                ins.then_inc(s_pe, 1)
                t.wait_ge(s_sc, sc_rstd(tt))
                if tt >= 2:
                    t.wait_ge(s_ve, ve0 + [3, 5, 7, 8][tt - 2])
                onr = A.f32("onesrf", 1, P)
                t.matmul(ps_m[tt % 2][:, :], onr, A.f32(f"mrow{tt % 2}", 1, TTW),
                         start=True, stop=True)
                t.matmul(ps_r[tt % 2][:, :], onr, A.f32(f"rrow{tt % 2}", 1, TTW),
                         start=True, stop=True).then_inc(s_pe, 1)
        C["pe"] = pe0 + 8

        VE_OPS = [1, 2, 4, 6]
        VE_XN = [3, 5, 7, 8]

        @blk.vector
        def _(v):
            def stats_ops(tt):
                v.wait_ge(s_pe, pe0 + 2 * tt + 1)
                mr = A.f32(f"mrow{tt % 2}", 1, TTW)
                ex = A.f32(f"exrow{tt % 2}", 1, TTW)
                var = A.f32(f"var{tt % 2}", 1, TTW)
                v.tensor_scalar_mul(mr, ps_sum[tt % 2][0:1, :], 1.0 / D)
                v.tensor_scalar_mul(ex, ps_sq[tt % 2][0:1, :], 1.0 / D)
                v.tensor_tensor(var, mr, mr, ALU.mult)
                v.tensor_tensor(var, ex, var, ALU.subtract).then_inc(s_ve, 1)

            def xn_ops(tt):
                v.wait_ge(s_pe, pe0 + 2 * tt + 2)
                v.wait_ge(s_xt[tt % 2], 16 * (tt // 2 + 1))
                for kc in range(KC):
                    tmp = A.f32(f"tmpA{kc % 2}", P, TTW)
                    v.tensor_tensor(tmp, A.f32(f"xt{tt % 3}", P, TTW, col_off=kc * TTW),
                                    ps_m[tt % 2][:, :], ALU.subtract)
                    ins = v.tensor_tensor(A.bf("xn", P, TTW, col_off=kc * T + tt * TTW),
                                          tmp, ps_r[tt % 2][:, :], ALU.mult)
                ins.then_inc(s_ve, 1)

            stats_ops(0); stats_ops(1); xn_ops(0); stats_ops(2)
            xn_ops(1); stats_ops(3); xn_ops(2); xn_ops(3)
        C["ve"] = ve0 + 8

    @blk.sync
    def _(sy):
        sy.wait_ge(s_ve, ve0 + 8)
        sy.dma_start(out=xn_hbm[:, :], in_=A.bf("xn", P, KC * T)).then_inc(s_xnst, 16)

    if PHASES < 2:
        @blk.sync
        def _(sy):
            sy.wait_ge(s_xnst, 16)
        return
    A.free("xt0"); A.free("xt1"); A.free("xt2")
    A.free("xsq0"); A.free("xsq1"); A.free("tmpA0"); A.free("tmpA1")
    for nm in ["mrow0", "mrow1", "rrow0", "rrow1", "exrow0", "exrow1", "var0", "var1"]:
        A.free(nm)
    A.alloc("qrope", KC * T * 2)
    A.alloc("krope", KC * T * 2)
    A.alloc("vst", NMC * VROW * 2)

    pe0, sc0, ve0 = C["pe"], C["sc"], C["ve"]

    @blk.gpsimd
    def _(g):
        g.wait_ge(s_ve, ve0)
        g.wait_ge(s_xt[0], 32)
        g.wait_ge(s_xt[1], 32)
        g.memset(A.bf("vst", P, NMC * VROW), 1.0)
        g.wait_ge(s_pe, pe0 + 2 * 16)
        g.dma_start(out=A.bf("w0", P, KC * D), in_=wvL[:, :]).then_inc(s_w[0], 16)

    rope_dst = {0: "qrope", 1: "krope"}

    with (
        nc.psum_tensor("ps_qk0", [P, QBW], F32) as ps_qk0,
        nc.psum_tensor("ps_qk1", [P, QBW], F32) as ps_qk1,
        nc.psum_tensor("ps_qs0", [P, QBW], F32) as ps_qs0,
        nc.psum_tensor("ps_qs1", [P, QBW], F32) as ps_qs1,
    ):
        ps_qk = [ps_qk0, ps_qk1]
        ps_qs = [ps_qs0, ps_qs1]

        def units():
            u = 0
            for p in range(2):
                for oc in range(KC):
                    for th in range(NQB):
                        yield u, p, oc, th
                        u += 1

        @blk.tensor
        def _(t):
            t.wait_ge(s_w[0], 16)
            t.wait_ge(s_w[1], 16)
            t.wait_ge(s_ve, ve0)
            def swap_mm(u):
                t.wait_ge(s_sc, sc0 + u + 1)
                if u >= 2:
                    t.wait_ge(s_ve, ve0 + (u - 2) + 1)
                for half in range(2):
                    ins = t.matmul(ps_qs[u % 2][:, half * 512:(half + 1) * 512],
                                   A.bf("swp", P, P),
                                   A.bf(f"qb{u % 2}", P, 512, col_off=half * 512),
                                   start=True, stop=True)
                ins.then_inc(s_pe, 1)

            for u, p, oc, th in units():
                if u >= 2:
                    t.wait_ge(s_sc, sc0 + (u - 2) + 1)
                wslot = f"w{p}"
                for kc in range(KC):
                    for half in range(2):
                        ins = t.matmul(
                            ps_qk[u % 2][:, half * 512:(half + 1) * 512],
                            A.bf(wslot, P, P, col_off=kc * D + oc * P),
                            A.bf("xn", P, 512,
                                 col_off=kc * T + th * QBW + half * 512),
                            start=(kc == 0), stop=(kc == KC - 1))
                ins.then_inc(s_pe, 1)
                if u >= 1:
                    swap_mm(u - 1)
            swap_mm(31)
            t.wait_ge(s_w[0], 32)
            t.wait_ge(s_sc, sc0 + 32)
            for m in range(NMC):
                if m >= 2:
                    t.wait_ge(s_sc, sc0 + 32 + (m - 2) + 1)
                for kc in range(KC):
                    for half in range(2):
                        t.matmul(ps_qk[m % 2][:, half * 512:(half + 1) * 512],
                                 A.bf("xn", P, P, col_off=kc * T + m * P),
                                 A.bf("w0", P, 512, col_off=kc * D + half * 512),
                                 start=(kc == 0), stop=False)
                for half in range(2):
                    ins = t.matmul(ps_qk[m % 2][:, half * 512:(half + 1) * 512],
                                   A.bf("onesrb", 1, P),
                                   A.bf("bvrow", 1, 512, col_off=half * 512),
                                   start=False, stop=True)
                ins.then_inc(s_pe, 1)
        C["pe"] = pe0 + 2 * 32 + 16

        @blk.scalar
        def _(sc):
            for u, p, oc, th in units():
                sc.wait_ge(s_pe, pe0 + (1 if u == 0 else 2 * u))
                if u >= 2:
                    sc.wait_ge(s_ve, ve0 + (u - 2) + 1)
                sc.activation(A.bf(f"qb{u % 2}", P, QBW), ps_qk[u % 2][:, :], AF.Identity,
                              bias=A.f32("bias4", P, 1, col_off=p * 8 + oc)).then_inc(s_sc, 1)
            for m in range(NMC):
                sc.wait_ge(s_pe, pe0 + 64 + m + 1)
                voff = A.live["vst"][0] // 2 + m * VROW
                dstap = bass.AP(A.hbf, voff, [[A.total // 2, P], [65, NH], [1, DH]])
                srcap = bass.AP(ps_qk[m % 2], 0, [[QBW, P], [DH, NH], [1, DH]])
                sc.activation(dstap, srcap, AF.Copy).then_inc(s_sc, 1)
        C["sc"] = sc0 + 32 + 16

        @blk.vector
        def _(v):
            for u, p, oc, th in units():
                v.wait_ge(s_sc, sc0 + u + 1)
                t1 = A.bf(f"t1s{u % 2}", P, QBW)
                t2 = A.bf(f"t2s{u % 2}", P, QBW)
                v.tensor_tensor(t1, A.bf(f"qb{u % 2}", P, QBW),
                                A.bf("tabs", P, QBW, col_off=th * QBW), ALU.mult)
                v.wait_ge(s_pe, pe0 + (2 * u + 3 if u < 31 else 64))
                v.tensor_tensor(t2, ps_qs[u % 2][:, :],
                                A.bf("tabs", P, QBW, col_off=T + th * QBW), ALU.mult)
                v.tensor_tensor(A.bf(rope_dst[p], P, QBW, col_off=oc * T + th * QBW),
                                t1, t2, ALU.add).then_inc(s_ve, 1)
        C["ve"] = ve0 + 32

    if PHASES < 3:
        @blk.sync
        def _(sy):
            sy.wait_ge(s_xnst, 16)
        return
    peA, scA, veA = C["pe"], C["sc"], C["ve"]
    A.free("w0"); A.free("w1")
    A.free("qb0"); A.free("qb1"); A.free("t1s0"); A.free("t1s1")
    A.free("t2s0"); A.free("t2s1")
    A.free("xn")
    A.free("tabs")
    A.free("swp")
    A.free("onesrf")
    A.free("bvrow")
    A.alloc("xnC0", KC * TTW * 2)
    A.alloc("xnC1", KC * TTW * 2)
    A.alloc("wis0", KC * P * 2)
    A.alloc("wis1", KC * P * 2)
    A.alloc("a01b", 2 * P * 2)
    A.alloc("b0p", FF * 2)
    A.alloc("b1b", FF * 2)
    A.alloc("xas0", TTW * 2)
    A.alloc("xas1", TTW * 2)
    A.alloc("xas0n", TTW * 2)
    A.alloc("h0s0", TTW * 2)
    A.alloc("h0s1", TTW * 2)
    A.alloc("h1s0", TTW * 2)
    A.alloc("h1s1", TTW * 2)
    A.alloc("hch0", TTW * 2)
    A.alloc("hch1", TTW * 2)
    A.alloc("hC0r", FC * TTW * 2)
    A.alloc("attn0", T * 2)
    A.alloc("attn1", T * 2)
    A.alloc("pt0", QBW * 2)
    A.alloc("pt1", QBW * 2)
    A.alloc("sumsb", QBW * 4)
    A.alloc("recf", 512 * 4)

    pei_pos = {}
    _k = 0
    for _tt in range(NTT):
        _k += 1; pei_pos[(_tt, "xA", 0)] = _k
        _k += 1; pei_pos[(_tt, "xA", 1)] = _k
        for _fc in range(FC):
            _k += 1; pei_pos[(_tt, "I", _fc)] = _k
            if _fc >= 1:
                _k += 1; pei_pos[(_tt, "D", _fc - 1)] = _k
        _k += 1; pei_pos[(_tt, "D", FC - 1)] = _k
    PEI_TOT = _k
    vei_pos = {}
    _k = 0
    for _tt in range(NTT):
        for _fc in range(FC):
            _k += 1; vei_pos[(_tt, "h0", _fc)] = _k
            if _fc >= 1:
                _k += 1; vei_pos[(_tt, "h1", _fc - 1)] = _k
                _k += 1; vei_pos[(_tt, "hadd", _fc - 1)] = _k
        _k += 1; vei_pos[(_tt, "h1", FC - 1)] = _k
        _k += 1; vei_pos[(_tt, "hadd", FC - 1)] = _k
    VEI_TOT = _k

    @blk.gpsimd
    def _(g):
        g.wait_ge(s_pe, peA)
        g.wait_ge(s_ve, veA)
        g.wait_ge(s_xnst, 16)
        g.memset(A.bf("xas0", P, TTW), 0.0)
        g.memset(A.bf("xas1", P, TTW), 0.0)
        g.memset(A.bf("xas0n", P, TTW), 0.0)
        g.memset(A.bf("b0p", P, FF), 0.0)
        g.memset(A.bf("b1b", P, FF), 0.0)
        g.dma_start(out=A.bf("a01b", P, 2 * P), in_=a01[:, :]).then_inc(s_cld, 16)
        g.dma_start(out=A.bf("b0p", RANK, FF), in_=b0T[:, :]).then_inc(s_cld, 16)
        g.dma_start(out=A.bf("b1b", RANK, FF), in_=b1T[:, :]).then_inc(s_cld, 16)
        for tt in range(2):
            srcx = bass.AP(xn_hbm, tt * TTW, [[KC * T, P], [T, KC], [1, TTW]])
            g.dma_start(out=A.bf(f"xnC{tt % 2}", P, KC * TTW),
                        in_=srcx).then_inc(s_xnin, 16)
        for idx in range(NTT * FC):
            tt, fc = divmod(idx, FC)
            if fc == 0 and tt >= 2:
                g.wait_ge(s_pei, pei_pos[(tt - 2, "D", FC - 1)])
                srcx = bass.AP(xn_hbm, tt * TTW, [[KC * T, P], [T, KC], [1, TTW]])
                g.dma_start(out=A.bf(f"xnC{tt % 2}", P, KC * TTW),
                            in_=srcx).then_inc(s_xnin, 16)
            if idx >= 2:
                tp, fp = divmod(idx - 2, FC)
                g.wait_ge(s_pei, pei_pos[(tp, "I", fp)])
            srcw = bass.AP(wiL, fc * P, [[KC * FF, P], [FF, KC], [1, P]])
            g.dma_start(out=A.bf(f"wis{idx % 2}", P, KC * P),
                        in_=srcw).then_inc(s_wis[idx % 2], 16)

    pe0, sc0, ve0 = peA, scA, veA
    QW = 512

    def pe_S_i(i):
        return pe0 + i + 1

    inter_pe = []

    with (
        nc.psum_tensor("ps_S0", [P, QBW], F32) as ps_S0,
        nc.psum_tensor("ps_S1", [P, QBW], F32) as ps_S1,
        nc.psum_tensor("ps_Oa", [65, QW], F32) as ps_Oa,
        nc.psum_tensor("ps_Ob", [65, QW], F32) as ps_Ob,
        nc.psum_tensor("ps_I0", [P, TTW], F32) as ps_I0,
        nc.psum_tensor("ps_I1", [P, TTW], F32) as ps_I1,
    ):
        ps_S = [ps_S0, ps_S1]
        ps_O = [ps_Oa, ps_Ob]
        ps_I = [ps_I0, ps_I1]

        def mk_xA(tt, e):
            def emit_half(hf):
                def emit(t):
                    if hf == 0:
                        if e == 0:
                            t.wait_ge(s_xnin, 16 * (tt + 1))
                        if tt >= 1:
                            t.wait_ge(s_vei, vei_pos[(tt - 1, "h1", FC - 2 + e)])
                    ins = None
                    for kc in range(hf * 4, hf * 4 + 4):
                        ins = t.matmul(ps_I[e][0:RANK, :],
                                       A.bf("a01b", P, RANK, col_off=e * P + kc * RANK),
                                       A.bf(f"xnC{tt % 2}", P, TTW, col_off=kc * TTW),
                                       start=(kc == 0), stop=(kc == KC - 1))
                    if hf == 1:
                        ins.then_inc(s_pei, 1)
                return emit
            inter_pe.append((4, (tt, "xA", e, 0), emit_half(0)))
            inter_pe.append((4, (tt, "xA", e), emit_half(1)))

        def mk_I(tt, fc):
            idx = tt * FC + fc
            def emit_half(hf):
                def emit(t):
                    if hf == 0:
                        t.wait_ge(s_wis[idx % 2], 16 * (idx // 2 + 1))
                        if fc == 0:
                            t.wait_ge(s_sci, tt * 3 + 1)
                        elif fc == 1:
                            t.wait_ge(s_sci, tt * 3 + 2)
                        if fc >= 2:
                            t.wait_ge(s_vei, vei_pos[(tt, "h1", fc - 2)])
                    for kc in range(hf * 4, hf * 4 + 4):
                        t.matmul(ps_I[fc % 2][:, :],
                                 A.bf(f"wis{idx % 2}", P, P, col_off=kc * P),
                                 A.bf(f"xnC{tt % 2}", P, TTW, col_off=kc * TTW),
                                 start=(kc == 0), stop=False)
                    if hf == 1:
                        t.matmul(ps_I[fc % 2][:, :],
                                 A.bf("b0p", P, P, col_off=fc * P),
                                 A.bf("xas0", P, TTW),
                                 start=False, stop=True).then_inc(s_pei, 1)
                return emit
            inter_pe.append((4, (tt, "I", fc, 0), emit_half(0)))
            inter_pe.append((5, (tt, "I", fc), emit_half(1)))

        def mk_D(tt, fc):
            def emit(t):
                t.wait_ge(s_vei, vei_pos[(tt, "h0", fc)])
                if fc == 0:
                    t.wait_ge(s_sci, tt * 3 + 3)
                t.matmul(ps_I[fc % 2][:, :],
                         A.bf("b1b", P, P, col_off=fc * P),
                         A.bf("xas1", P, TTW), start=False, stop=False,
                         skip_group_check=True)
                t.matmul(ps_I[fc % 2][:, :],
                         A.bf("b0p", P, P, col_off=fc * P),
                         A.bf("xas0n", P, TTW),
                         start=False, stop=True,
                         skip_group_check=True).then_inc(s_pei, 1)
            inter_pe.append((2, (tt, "D", fc), emit))

        for tt in range(NTT):
            mk_xA(tt, 0)
            mk_xA(tt, 1)
            for fc in range(FC):
                mk_I(tt, fc)
                if fc >= 1:
                    mk_D(tt, fc - 1)
            mk_D(tt, FC - 1)
        INTER_MM = sum(n for n, _, _ in inter_pe)
        step_of = {}

        @blk.tensor
        def _(t):
            t.wait_ge(s_ve, veA)
            t.wait_ge(s_sc, scA)
            t.wait_ge(s_cld, 48)

            def S_pair(i, inc):
                g, m = divmod(i, 16)
                c, qb4 = divmod(g, 4)
                ins = None
                for hh in range(2):
                    ins = t.matmul(
                        ps_S[i % 2][:, hh * QW:(hh + 1) * QW],
                        A.bf("krope", DH, P, row_off=hh * 64, col_off=c * T + m * P),
                        A.bf("qrope", DH, QW, row_off=hh * 64,
                             col_off=c * T + qb4 * QW),
                        start=True, stop=True)
                if inc:
                    ins.then_inc(s_pe2, 1)
                return ins

            for i in range(2):
                S_pair(i, True)
            iop = 0
            emitted = 0
            for i in range(512):
                g, m = divmod(i, 16)
                c, qb4 = divmod(g, 4)
                t.wait_ge(s_sc, sc0 + i + 1)
                if m == 0 and g >= 1:
                    t.wait_ge(s_ve, ve0 + 2 * g - 1)
                if i + 2 < 512:
                    S_pair(i + 2, True)
                for hh in range(2):
                    ins = t.matmul(
                        ps_O[hh][:, :],
                        A.bf("vst", P, 65, col_off=m * VROW + (2 * c + hh) * 65),
                        A.bf(f"pt{i % 2}", P, QW, col_off=hh * QW),
                        start=(m == 0), stop=(m == NMC - 1))
                ins.then_inc(s_pe, 1)
                target = (i + 1) * INTER_MM / 512.0
                while iop < len(inter_pe) and emitted < target:
                    n, key, fn = inter_pe[iop]
                    fn(t)
                    step_of[key] = i
                    emitted += n
                    iop += 1
            while iop < len(inter_pe):
                n, key, fn = inter_pe[iop]
                fn(t)
                step_of[key] = 511
                iop += 1
        C["pe"] = pe0 + 512

        def step_c(i):
            return divmod(divmod(i, 16)[0], 4)[0]

        sc_inserts = {}
        for tt in range(NTT):
            st = min(step_of[(tt, "xA", 1)] + 2, 511)
            def mk_xas(tt):
                def emit(sc):
                    sc.wait_ge(s_pei, pei_pos[(tt, "xA", 0)])
                    if tt >= 1:
                        sc.wait_ge(s_pei, pei_pos[(tt - 1, "I", FC - 1)])
                    sc.activation(A.bf("xas0", RANK, TTW), ps_I[0][0:RANK, :],
                                  AF.Copy).then_inc(s_sci, 1)
                    sc.wait_ge(s_pei, pei_pos[(tt, "xA", 1)])
                    if tt >= 1:
                        sc.wait_ge(s_pei, pei_pos[(tt - 1, "D", FC - 1)])
                    sc.activation(A.bf("xas1", RANK, TTW), ps_I[1][0:RANK, :],
                                  AF.Copy).then_inc(s_sci, 1)
                    sc.activation(A.bf("xas0n", RANK, TTW), A.bf("xas0", RANK, TTW),
                                  AF.Identity,
                                  scale=A.f32("cst", RANK, 1, col_off=4)).then_inc(s_sci, 1)
                return emit
            sc_inserts.setdefault(st, []).append(mk_xas(tt))

        @blk.scalar
        def _(sc):
            for i in range(512):
                sc.wait_ge(s_pe2, i + 1)
                if i >= 2:
                    sc.wait_ge(s_pe, pe0 + i - 1)
                sc.activation(A.bf(f"pt{i % 2}", P, QBW), ps_S[i % 2][:, :],
                              AF.Exp, bias=A.f32("cst", P, 1, col_off=1),
                              scale=A.f32("cst", P, 1, col_off=0)).then_inc(s_sc, 1)
                for fn in sc_inserts.get(i, []):
                    fn(sc)
        C["sc"] = sc0 + 512

        ve_inserts = {}

        def add_ve(st, emit):
            ve_inserts.setdefault(min(st, 511), []).append(emit)

        def mk_h0(tt, fc):
            def emit(v):
                v.wait_ge(s_pei, pei_pos[(tt, "I", fc)])
                v.tensor_scalar(A.bf(f"h0s{fc % 2}", P, TTW), ps_I[fc % 2][:, :],
                                A.f32("tkv", P, 1, col_off=0), 0.0,
                                ALU.mult, ALU.max).then_inc(s_vei, 1)
            return emit

        def mk_h1(tt, fc):
            def emit(v):
                v.wait_ge(s_pei, pei_pos[(tt, "D", fc)])
                v.tensor_scalar(A.bf(f"h1s{fc % 2}", P, TTW), ps_I[fc % 2][:, :],
                                A.f32("tkv", P, 1, col_off=1), 0.0,
                                ALU.mult, ALU.max).then_inc(s_vei, 1)
            return emit

        def mk_hadd(tt, fc):
            j = (tt - 1) * FC + fc
            def emit(v):
                if tt == 0:
                    dst = A.bf("hC0r", P, TTW, col_off=fc * TTW)
                else:
                    if j >= 2:
                        v.wait_ge(s_hst[j % 2], 16 * (j // 2))
                    dst = A.bf(f"hch{j % 2}", P, TTW)
                v.tensor_tensor(dst,
                                A.bf(f"h0s{fc % 2}", P, TTW),
                                A.bf(f"h1s{fc % 2}", P, TTW),
                                ALU.add).then_inc(s_vei, 1)
            return emit

        for tt in range(NTT):
            for fc in range(FC):
                add_ve(step_of[(tt, "I", fc)], mk_h0(tt, fc))
                if fc >= 1:
                    add_ve(step_of[(tt, "D", fc - 1)], mk_h1(tt, fc - 1))
                    add_ve(step_of[(tt, "D", fc - 1)], mk_hadd(tt, fc - 1))
            add_ve(step_of[(tt, "D", FC - 1)], mk_h1(tt, FC - 1))
            add_ve(step_of[(tt, "D", FC - 1)], mk_hadd(tt, FC - 1))

        @blk.vector
        def _(v):
            for i in range(512):
                g, m = divmod(i, 16)
                if m == 15:
                    c, qb4 = divmod(g, 4)
                    v.wait_ge(s_pe, pe0 + g * 16 + 16)
                    if c >= 2 and qb4 == 0:
                        v.wait_ge(s_ast[c % 2], 16 * (c // 2))
                    for hh in range(2):
                        v.tensor_copy(
                            A.bf(f"attn{c % 2}", 64, QW, row_off=hh * 64,
                                 col_off=qb4 * QW),
                            ps_O[hh][0:64, :])
                    ins = None
                    for hh in range(2):
                        ins = v.tensor_copy(A.f32("sumsb", 1, QW, col_off=hh * QW),
                                            ps_O[hh][64:65, :])
                    ins.then_inc(s_ve, 1)
                    if g >= 1:
                        v.wait_ge(s_rec, 16 * 2 * g)
                    for hh in range(2):
                        ins = v.tensor_copy(A.bf("recf", 1, QW, col_off=hh * QW),
                                            A.f32("sumsb", 1, QW, col_off=hh * QW))
                    ins.then_inc(s_ve, 1)
                for fn in ve_inserts.get(i, []):
                    fn(v)
        C["ve"] = ve0 + 2 * 32

        sy_inserts = {}
        for tt in range(1, NTT):
            for fc in range(FC):
                j = (tt - 1) * FC + fc
                def mk_hst(tt, fc, j):
                    def emit(sy):
                        sy.wait_ge(s_vei, vei_pos[(tt, "hadd", fc)])
                        sy.dma_start(
                            out=h_hbm[:, fc * T + tt * TTW: fc * T + tt * TTW + TTW],
                            in_=A.bf(f"hch{j % 2}", P, TTW)
                        ).then_inc(s_hst[j % 2], 16)
                    return emit
                st = min(step_of[(tt, "D", fc)] + 2, 511)
                sy_inserts.setdefault(st, []).append(mk_hst(tt, fc, j))

        @blk.sync
        def _(sy):
            for i in range(512):
                g, m = divmod(i, 16)
                if m == 15:
                    c, qb4 = divmod(g, 4)
                    sy.wait_ge(s_ve, ve0 + 2 * g + 2)
                    for hh in range(2):
                        sy.dma_start(
                            out=rec_hbm[2 * c + hh:2 * c + hh + 1,
                                        qb4 * QW:(qb4 + 1) * QW],
                            in_=A.bf("recf", 1, QW, col_off=hh * QW)
                        ).then_inc(s_rec, 16)
                    if qb4 == 3:
                        sy.wait_ge(s_ve, ve0 + 2 * g + 1)
                        sy.dma_start(out=env["at_hbm"][:, c * T:(c + 1) * T],
                                     in_=A.bf(f"attn{c % 2}", P, T)
                                     ).then_inc(s_ast[c % 2], 16)
                for fn in sy_inserts.get(i, []):
                    fn(sy)

    if PHASES < 4:
        @blk.sync
        def _(sy):
            sy.wait_ge(s_ast[0], 64)
            sy.wait_ge(s_ast[1], 64)
            sy.wait_ge(s_hst[0], 16 * 16 * (NTT - 1))
            sy.wait_ge(s_hst[1], 16 * 16 * (NTT - 1))
        return
    peB2, scB2, veB2 = C["pe"], C["sc"], C["ve"]
    A.free("qrope"); A.free("krope"); A.free("vst")
    A.free("pt0"); A.free("pt1")
    A.free("sumsb"); A.free("recf")
    A.free("attn0"); A.free("attn1")
    A.free("xnC0"); A.free("xnC1"); A.free("wis0"); A.free("wis1")
    A.free("a01b"); A.free("b0p"); A.free("b1b")
    A.free("xas0"); A.free("xas1"); A.free("xas0n")
    A.free("h0s0"); A.free("h0s1"); A.free("h1s0"); A.free("h1s1")
    A.free("hch0"); A.free("hch1")
    A.alloc("rec_all", T * 2)
    A.alloc("recallf", T * 4)
    A.alloc("woP", KC * D * 2)
    A.alloc("hC0", FC * TTW * 2)
    A.alloc("hC1", FC * TTW * 2)
    A.alloc("wos0", FF * 2)
    A.alloc("wos1", FF * 2)
    A.alloc("wos2", FF * 2)
    A.alloc("atC0", KC * TTW * 2)
    A.alloc("atC1", KC * TTW * 2)
    A.alloc("xcc0", TTW * 4)
    A.alloc("xcc1", TTW * 4)
    A.alloc("outc0", TTW * 4)
    A.alloc("outc1", TTW * 4)
    A.alloc("t2s0c", TTW * 4)
    A.alloc("t2s1c", TTW * 4)

    pe_pos2 = {}
    _k = 0
    for _j in range(KC):
        _k += 1; pe_pos2[(0, "rb", _j)] = _k
    for _tt in range(NTT):
        for _j in range(KC):
            _k += 1; pe_pos2[(_tt, "F", _j)] = _k
            if _tt + 1 < NTT:
                _k += 1; pe_pos2[(_tt + 1, "rb", _j)] = _k
    PE2_TOT = _k
    ve_pos2 = {}
    _k = 2
    ve_pos2[("rec", 0)] = 2
    for _j in range(KC):
        _k += 1; ve_pos2[(0, "rbTT", _j)] = _k
    for _tt in range(NTT):
        if _tt + 1 < NTT:
            _k += 2; ve_pos2[("rec", _tt + 1)] = _k
        for _j in range(KC):
            _k += 1; ve_pos2[(_tt, "out", _j)] = _k
            if _tt + 1 < NTT:
                _k += 1; ve_pos2[(_tt + 1, "rbTT", _j)] = _k
    VE2_TOT = _k

    def peC2(tt, key, j):
        return peB2 + pe_pos2[(tt, key, j)]

    def scC2(tt, oc):
        return scB2 + tt * KC + oc + 1

    def veC2(tt, key, j):
        return veB2 + ve_pos2[(tt, key, j)]

    @blk.gpsimd
    def _(g):
        g.wait_ge(s_pe, peB2)
        g.wait_ge(s_vei, VEI_TOT)
        g.wait_ge(s_hst[0], 16 * 16 * (NTT - 1))
        g.wait_ge(s_hst[1], 16 * 16 * (NTT - 1))
        g.wait_ge(s_ast[0], 64)
        g.wait_ge(s_ast[1], 64)
        g.wait_ge(s_rec, 16 * 64)
        g.dma_start(out=A.bf("rec_all", RANK, T), in_=rec_hbm[:, :]).then_inc(s_cld, 16)
        g.dma_start(out=A.bf("woP", P, KC * D), in_=woPL[:, :]).then_inc(s_wop, 16)
        for tt in range(NTT):
            for oc in range(KC):
                idx = tt * KC + oc
                if idx >= 3:
                    t3p, o3p = divmod(idx - 3, KC)
                    g.wait_ge(s_pe, peC2(t3p, "F", o3p))
                for hf in range(2):
                    g.dma_start(out=A.bf(f"wos{idx % 3}", P, FF // 2, col_off=hf * FF // 2),
                                in_=woL[:, oc * FF + hf * FF // 2:
                                        oc * FF + (hf + 1) * FF // 2]).then_inc(s_wos, 16)

    @blk.sync
    def _(sy):
        sy.wait_ge(s_pe, peB2)
        sy.wait_ge(s_vei, VEI_TOT)
        sy.wait_ge(s_hst[0], 16 * 16 * (NTT - 1))
        sy.wait_ge(s_hst[1], 16 * 16 * (NTT - 1))
        sy.wait_ge(s_ast[0], 64)
        sy.wait_ge(s_ast[1], 64)
        sy.wait_ge(s_rec, 16 * 64)

        def loads(tt):
            if tt >= 2:
                sy.wait_ge(s_pe, peC2(tt - 2, "F", KC - 1))
            src = bass.AP(at_hbm, tt * TTW, [[KC * T, P], [T, KC], [1, TTW]])
            sy.dma_start(out=A.bf(f"atC{tt % 2}", P, KC * TTW),
                         in_=src).then_inc(s_att[tt % 2], 16)

        def loads_h(tt):
            if tt >= 3:
                sy.wait_ge(s_pe, peC2(tt - 2, "F", KC - 1))
            srch = bass.AP(h_hbm, tt * TTW, [[FC * T, P], [T, FC], [1, TTW]])
            sy.dma_start(out=A.bf(f"hC{tt % 2}", P, FC * TTW),
                         in_=srch).then_inc(s_hld, 16)

        def xcc_load(idx2):
            tt2, oc2 = divmod(idx2, KC)
            sy.dma_start(out=A.f32(f"xcc{idx2 % 2}", P, TTW),
                         in_=xR[:, oc2 * T + tt2 * TTW: oc2 * T + tt2 * TTW + TTW]
                         ).then_inc(s_xc[idx2 % 2], 16)

        def stores(tt):
            for oc in range(KC):
                idx = tt * KC + oc
                sy.wait_ge(s_ve, veC2(tt, "out", oc))
                sy.dma_start(out=yT[oc * P:(oc + 1) * P, tt * TTW:(tt + 1) * TTW],
                             in_=A.f32(f"outc{idx % 2}", P, TTW)).then_inc(s_yst[idx % 2], 16)
                if idx + 2 < NTT * KC:
                    xcc_load(idx + 2)

        xcc_load(0)
        xcc_load(1)
        loads(0)
        loads(1)
        loads_h(1)
        loads_h(2)
        for tt in range(1, NTT):
            stores(tt - 1)
            if tt + 1 < NTT:
                loads(tt + 1)
            if tt == 2:
                loads_h(3)
        stores(NTT - 1)
        sy.wait_ge(s_yst[0], 16 * 16)
        sy.wait_ge(s_yst[1], 16 * 16)

    with (
        nc.psum_tensor("ps_F0", [P, TTW], F32) as ps_F0,
        nc.psum_tensor("ps_F1", [P, TTW], F32) as ps_F1,
        nc.psum_tensor("ps_rb0", [P, TTW], F32) as ps_rb0,
        nc.psum_tensor("ps_rb1", [P, TTW], F32) as ps_rb1,
    ):
        ps_F = [ps_F0, ps_F1]
        ps_rb = [ps_rb0, ps_rb1]

        @blk.tensor
        def _(t):
            t.wait_ge(s_wop, 16)

            def rb_mm(tt, fc):
                if fc == 0:
                    t.wait_ge(s_ve, veB2 + ve_pos2[("rec", tt)])
                if fc >= 2:
                    t.wait_ge(s_ve, veC2(tt, "rbTT", fc - 2))
                elif tt >= 1:
                    t.wait_ge(s_ve, veC2(tt - 1, "rbTT", KC - 2 + fc))
                t.matmul(ps_rb[fc % 2][:, :], A.bf("sel16", P, P, col_off=fc * P),
                         A.bf("rec_all", P, TTW, col_off=tt * TTW),
                         start=True, stop=True).then_inc(s_pe, 1)

            for fc in range(KC):
                rb_mm(0, fc)
            for tt in range(NTT):
                t.wait_ge(s_ve, veC2(tt, "rbTT", KC - 1))
                if tt == 0:
                    t.wait_ge(s_vei, vei_pos[(0, "hadd", FC - 1)])
                else:
                    t.wait_ge(s_hld, 16 * tt)
                for oc in range(KC):
                    idx = tt * KC + oc
                    if oc >= 2:
                        t.wait_ge(s_sc, scC2(tt, oc - 2))
                    elif tt >= 1:
                        t.wait_ge(s_sc, scC2(tt - 1, KC - 2 + oc))
                    t.wait_ge(s_wos, 32 * (idx + 1))
                    for kc in range(KC):
                        t.matmul(ps_F[oc % 2][:, :],
                                 A.bf("woP", P, P, col_off=kc * D + oc * P),
                                 A.bf(f"atC{tt % 2}", P, TTW, col_off=kc * TTW),
                                 start=(kc == 0), stop=False)
                    hsrc = "hC0r" if tt == 0 else f"hC{tt % 2}"
                    for fc in range(FC):
                        ins = t.matmul(ps_F[oc % 2][:, :],
                                       A.bf(f"wos{idx % 3}", P, P, col_off=fc * P),
                                       A.bf(hsrc, P, TTW, col_off=fc * TTW),
                                       start=False, stop=(fc == FC - 1))
                    ins.then_inc(s_pe, 1)
                    if tt + 1 < NTT:
                        rb_mm(tt + 1, oc)
        C["pe"] = peB2 + PE2_TOT

        @blk.scalar
        def _(sc):
            for tt in range(NTT):
                for oc in range(KC):
                    idx = tt * KC + oc
                    sc.wait_ge(s_pe, peC2(tt, "F", oc))
                    if idx >= 2:
                        t2p, o2p = divmod(idx - 2, KC)
                        sc.wait_ge(s_ve, veC2(t2p, "out", o2p))
                    sc.activation(A.f32(f"t2s{oc % 2}c", P, TTW), ps_F[oc % 2][:, :],
                                  AF.Identity,
                                  bias=A.f32("bias4", P, 1, col_off=24 + oc)).then_inc(s_sc, 1)
        C["sc"] = scB2 + NTT * KC

        @blk.vector
        def _(v):
            def rec_slice(tt):
                v.reciprocal(A.f32("recallf", RANK, TTW, col_off=tt * TTW),
                             A.bf("rec_all", RANK, TTW, col_off=tt * TTW)).then_inc(s_ve, 1)
                v.tensor_copy(A.bf("rec_all", RANK, TTW, col_off=tt * TTW),
                              A.f32("recallf", RANK, TTW, col_off=tt * TTW)).then_inc(s_ve, 1)

            def rbTT(tt, fc):
                v.wait_ge(s_pe, peC2(tt, "rb", fc))
                if fc == 0:
                    v.wait_ge(s_att[tt % 2], 16 * (tt // 2 + 1))
                v.tensor_tensor(A.bf(f"atC{tt % 2}", P, TTW, col_off=fc * TTW),
                                A.bf(f"atC{tt % 2}", P, TTW, col_off=fc * TTW),
                                ps_rb[fc % 2][:, :], ALU.mult).then_inc(s_ve, 1)

            v.wait_ge(s_cld, 64)
            rec_slice(0)
            for fc in range(KC):
                rbTT(0, fc)
            for tt in range(NTT):
                if tt + 1 < NTT:
                    rec_slice(tt + 1)
                for oc in range(KC):
                    idx = tt * KC + oc
                    v.wait_ge(s_sc, scC2(tt, oc))
                    v.wait_ge(s_xc[idx % 2], 16 * (idx // 2 + 1))
                    if idx >= 2:
                        v.wait_ge(s_yst[idx % 2], 16 * (idx // 2))
                    v.tensor_tensor(A.f32(f"outc{idx % 2}", P, TTW),
                                    A.f32(f"t2s{oc % 2}c", P, TTW),
                                    A.f32(f"xcc{idx % 2}", P, TTW), ALU.add).then_inc(s_ve, 1)
                    if tt + 1 < NTT:
                        rbTT(tt + 1, oc)
        C["ve"] = veB2 + VE2_TOT



def _rope_perm():
    perm = []
    for h in range(NH):
        perm += [h * DH + 2 * i for i in range(DH // 2)]
        perm += [h * DH + 2 * i + 1 for i in range(DH // 2)]
    return np.array(perm, dtype=np.int64)


def _chunk_tile(wT):
    n = wT.shape[1]
    return np.ascontiguousarray(wT.reshape(KC, P, n).transpose(1, 0, 2).reshape(P, KC * n))


def _tables():
    inv = 1.0 / (10000.0 ** (np.arange(0, DH, 2, dtype=np.float64) / DH))
    pos = np.arange(T, dtype=np.float64)
    th = pos[None, :] * inv[:, None]
    cos = np.cos(th).astype(np.float32)
    sin = np.sin(th).astype(np.float32)
    ccc = np.concatenate([cos, cos, cos, cos], axis=0)
    ssp = np.concatenate([-sin, sin, -sin, sin], axis=0)
    return np.ascontiguousarray(np.concatenate([ccc, ssp], axis=1))


def _sel16():
    s = np.zeros((RANK, KC * P), np.float32)
    for c in range(KC):
        for p in range(P):
            s[2 * c + (p >= 64), c * P + p] = 1.0
    return s


def _swap_mat():
    s = np.zeros((P, P), np.float32)
    for g in range(2):
        for i in range(32):
            s[g * 64 + i, g * 64 + 32 + i] = 1.0
            s[g * 64 + 32 + i, g * 64 + i] = 1.0
    return s


def make_common_inputs(Wq, bq, Wk, bk, Wv, bv, Wo, bo, wi_w, wo_w,
                       lora_As, lora_Bs, tki, tkvals):
    perm = _rope_perm()
    wq_p = Wq[perm]
    wk_p = Wk[perm]
    bq_p = bq[perm]
    bk_p = bk[perm]
    e0 = int(tki[0]); e1 = int(tki[1])
    d = {
        "wqL": _chunk_tile(np.ascontiguousarray(wq_p.T)),
        "wkL": _chunk_tile(np.ascontiguousarray(wk_p.T)),
        "wvL": _chunk_tile(np.ascontiguousarray(Wv.T)),
        "woPL": _chunk_tile(np.ascontiguousarray(Wo.T)),
        "wiL": _chunk_tile(np.ascontiguousarray(wi_w.T)),
        "woL": np.ascontiguousarray(
            wo_w.T.reshape(FC, P, KC, P).transpose(1, 2, 0, 3).reshape(P, KC * FF)),
        "a01": np.ascontiguousarray(
            np.stack([lora_As[e0].T, lora_As[e1].T], axis=0)
            .reshape(2, KC, P, RANK).transpose(2, 0, 1, 3).reshape(P, 2 * P)),
        "b0T": np.ascontiguousarray(lora_Bs[e0].T),
        "b1T": np.ascontiguousarray(lora_Bs[e1].T),
        "bias4": np.ascontiguousarray(
            np.stack([bq_p.reshape(KC, P), bk_p.reshape(KC, P),
                      bv.reshape(KC, P), bo.reshape(KC, P)], axis=0)
            .reshape(4 * KC, P).T),
        "bvrow": np.ascontiguousarray(bv[None, :]),
        "tkv": np.ascontiguousarray(np.broadcast_to(tkvals[None, :], (P, 2))),
        "tabs": _tables(),
        "swp": _swap_mat(),
        "onesrf": np.ones((1, P), np.float32),
        "sel16": _sel16(),
    }
    return {k: np.ascontiguousarray(v.astype(np.float32)) for k, v in d.items()}


def kernel(L0_attn_x, L0_attn_Wq, L0_attn_bq, L0_attn_Wk, L0_attn_bk,
           L0_attn_Wv, L0_attn_bv, L0_attn_Wo, L0_attn_bo,
           L0_ffn_wi_w, L0_ffn_wo_w, L0_ffn_lora_As, L0_ffn_lora_Bs,
           L0_ffn_top_k_indices, L0_ffn_top_k_values):
    x = np.asarray(L0_attn_x, dtype=np.float32)
    args = [np.asarray(a, dtype=np.float32) for a in
            (L0_attn_Wq, L0_attn_bq, L0_attn_Wk, L0_attn_bk, L0_attn_Wv,
             L0_attn_bv, L0_attn_Wo, L0_attn_bo, L0_ffn_wi_w, L0_ffn_wo_w,
             L0_ffn_lora_As, L0_ffn_lora_Bs)]
    tki = np.asarray(L0_ffn_top_k_indices)
    tkvals = np.asarray(L0_ffn_top_k_values, dtype=np.float32)
    (Wq, bq, Wk, bk, Wv, bv, Wo, bo, wi_w, wo_w, lora_As, lora_Bs) = args

    nc = build_core_program()
    common = make_common_inputs(Wq, bq, Wk, bk, Wv, bv, Wo, bo,
                                wi_w, wo_w, lora_As, lora_Bs, tki, tkvals)
    in_maps = []
    for c in range(8):
        m = dict(common)
        m["xR"] = _chunk_tile(np.ascontiguousarray(x[c].T))
        in_maps.append(m)
    res = run_bass_kernel_spmd(nc, in_maps, core_ids=list(range(8)))
    out = np.empty((8, T, D), np.float32)
    for c in range(8):
        out[c] = res.results[c]["yT"].T
    return out


# version 30
# speedup vs baseline: 1.0256x; 1.0085x over previous
import numpy as np
import ml_dtypes
import concourse.bass as bass
import concourse.mybir as mybir
from concourse.bass_utils import run_bass_kernel_spmd

F32 = mybir.dt.float32
BF16 = mybir.dt.bfloat16
U8 = mybir.dt.uint8
AF = mybir.ActivationFunctionType
ALU = mybir.AluOpType

P = 128
T = 2048
D = 1024
KC = 8
NH = 16
DH = 64
FF = 4096
FC = 32
NTT = 4
TTW = 512
NQB = 2
QBW = 1024
NMC = 16
VROW = NH * 65
RANK = 16

EXP_SCALE = 0.125
EXP_SHIFT = -4.0
LN_EPS = 1e-5
PHASES = 4


class Arena:

    def __init__(self, nc, total_bytes):
        self.total = total_bytes
        self.h8 = nc.alloc_sbuf_tensor("arena", [P, total_bytes], U8)
        self.hbf = self.h8.bitcast(BF16)
        self.hf32 = self.h8.bitcast(F32)
        self.free_list = [(0, total_bytes)]
        self.live = {}

    def alloc(self, name, nbytes):
        nbytes = (nbytes + 127) & ~127
        best = None
        for i, (off, sz) in enumerate(self.free_list):
            if sz >= nbytes and (best is None or sz < self.free_list[best][1]):
                best = i
        if best is None:
            raise MemoryError(f"arena: cannot alloc {name} ({nbytes}B); free={self.free_list}")
        off, sz = self.free_list[best]
        self.free_list[best] = (off + nbytes, sz - nbytes)
        if self.free_list[best][1] == 0:
            del self.free_list[best]
        self.live[name] = (off, nbytes)
        return off

    def free(self, name):
        off, sz = self.live.pop(name)
        self.free_list.append((off, sz))
        self.free_list.sort()
        merged = []
        for o, s in self.free_list:
            if merged and merged[-1][0] + merged[-1][1] == o:
                merged[-1] = (merged[-1][0], merged[-1][1] + s)
            else:
                merged.append((o, s))
        self.free_list = [tuple(m) for m in merged]

    def bf(self, name, p, n, row_off=0, col_off=0):
        off = self.live[name][0]
        assert off % 2 == 0
        return self.hbf[row_off:row_off + p, off // 2 + col_off: off // 2 + col_off + n]

    def f32(self, name, p, n, row_off=0, col_off=0):
        off = self.live[name][0]
        assert off % 4 == 0
        return self.hf32[row_off:row_off + p, off // 4 + col_off: off // 4 + col_off + n]


def build_core_program():
    nc = bass.Bass("TRN2", target_bir_lowering=False, detect_race_conditions=False)

    xR = nc.dram_tensor("xR", [P, KC * T], F32, kind="ExternalInput")
    wqL = nc.dram_tensor("wqL", [P, KC * D], F32, kind="ExternalInput")
    wkL = nc.dram_tensor("wkL", [P, KC * D], F32, kind="ExternalInput")
    wvL = nc.dram_tensor("wvL", [P, KC * D], F32, kind="ExternalInput")
    woPL = nc.dram_tensor("woPL", [P, KC * D], F32, kind="ExternalInput")
    wiL = nc.dram_tensor("wiL", [P, KC * FF], F32, kind="ExternalInput")
    woL = nc.dram_tensor("woL", [P, KC * FF], F32, kind="ExternalInput")
    a01 = nc.dram_tensor("a01", [P, 2 * 128], F32, kind="ExternalInput")
    b0T = nc.dram_tensor("b0T", [RANK, FF], F32, kind="ExternalInput")
    b1T = nc.dram_tensor("b1T", [RANK, FF], F32, kind="ExternalInput")
    bias4 = nc.dram_tensor("bias4", [P, 32], F32, kind="ExternalInput")
    bvrow = nc.dram_tensor("bvrow", [1, D], F32, kind="ExternalInput")
    tkv = nc.dram_tensor("tkv", [P, 2], F32, kind="ExternalInput")
    tabs = nc.dram_tensor("tabs", [P, 2 * T], F32, kind="ExternalInput")
    swp = nc.dram_tensor("swp", [P, P], F32, kind="ExternalInput")
    onesrf = nc.dram_tensor("onesrf", [1, P], F32, kind="ExternalInput")
    sel16 = nc.dram_tensor("sel16", [RANK, KC * P], F32, kind="ExternalInput")
    yT = nc.dram_tensor("yT", [D, T], F32, kind="ExternalOutput")
    xn_hbm = nc.dram_tensor("xn_hbm", [P, KC * T], BF16, kind="Internal")
    at_hbm = nc.dram_tensor("at_hbm", [P, KC * T], BF16, kind="Internal")
    rec_hbm = nc.dram_tensor("rec_hbm", [RANK, T], BF16, kind="Internal")
    h_hbm = nc.dram_tensor("h_hbm", [P, FC * T], BF16, kind="Internal")

    A = Arena(nc, (nc.sbuf_bytes_remaining - 1024) & ~127)

    A.alloc("bias4", 32 * 4)
    A.alloc("tkv", 2 * 4)
    A.alloc("cst", 8 * 4)
    A.alloc("swp", P * 2)
    A.alloc("onesrf", P * 4)
    A.alloc("onesrb", P * 2)
    A.alloc("onescb", 128)
    A.alloc("onescf", 128)
    A.alloc("bvrow", D * 2)
    A.alloc("tabs", 2 * T * 2)
    A.alloc("sel16", KC * P * 2)
    A.alloc("xn", KC * T * 2)

    from contextlib import ExitStack
    with ExitStack() as stk:
        names = ["s_cst", "s_w0", "s_w1", "s_wi", "s_wop", "s_wo0", "s_wo1",
                 "s_cld", "s_xt0", "s_xt1", "s_xnin", "s_att0", "s_att1",
                 "s_xc0", "s_xc1", "s_xnst", "s_ast0", "s_ast1", "s_yst0",
                 "s_yst1", "s_pe", "s_sc", "s_ve", "s_rec", "s_sq",
                 "s_pei", "s_sci", "s_vei", "s_wis0", "s_wis1",
                 "s_hst0", "s_hst1", "s_hld", "s_pe2", "s_wos"]
        S = {n: stk.enter_context(nc.semaphore(n)) for n in names}
        blk = stk.enter_context(nc.Block())

        env = dict(
            xR=xR, wqL=wqL, wkL=wkL, wvL=wvL, woPL=woPL, wiL=wiL, woL=woL,
            a01=a01, b0T=b0T, b1T=b1T, bias4=bias4, bvrow=bvrow, tkv=tkv,
            tabs=tabs, swp=swp, onesrf=onesrf, sel16=sel16, yT=yT, xn_hbm=xn_hbm,
            at_hbm=at_hbm, rec_hbm=rec_hbm,
            s_cst=S["s_cst"], s_wi=S["s_wi"], s_wop=S["s_wop"],
            s_cld=S["s_cld"], s_xnin=S["s_xnin"], s_xnst=S["s_xnst"],
            s_pe=S["s_pe"], s_sc=S["s_sc"], s_ve=S["s_ve"],
            s_xt=[S["s_xt0"], S["s_xt1"]], s_w=[S["s_w0"], S["s_w1"]],
            s_wo=[S["s_wo0"], S["s_wo1"]], s_att=[S["s_att0"], S["s_att1"]],
            s_xc=[S["s_xc0"], S["s_xc1"]], s_ast=[S["s_ast0"], S["s_ast1"]],
            s_yst=[S["s_yst0"], S["s_yst1"]], s_rec=S["s_rec"], s_sq=S["s_sq"],
            s_pei=S["s_pei"], s_sci=S["s_sci"], s_vei=S["s_vei"],
            s_wis=[S["s_wis0"], S["s_wis1"]],
            s_hst=[S["s_hst0"], S["s_hst1"]], s_hld=S["s_hld"],
            s_pe2=S["s_pe2"], s_wos=S["s_wos"],
            h_hbm=h_hbm,
        )
        C = dict(pe=0, sc=0, ve=0)
        with nc.allow_low_precision(reason="bf16 working precision by design"):
            _build_body(nc, blk, A, C, env)
    return nc


def _build_body(nc, blk, A, C, env):
    xR = env["xR"]; wqL = env["wqL"]; wkL = env["wkL"]; wvL = env["wvL"]
    woPL = env["woPL"]; wiL = env["wiL"]; woL = env["woL"]
    a01 = env["a01"]; b0T = env["b0T"]; b1T = env["b1T"]
    bias4 = env["bias4"]; bvrow = env["bvrow"]; tkv = env["tkv"]
    tabs = env["tabs"]; swp = env["swp"]; onesrf = env["onesrf"]; sel16 = env["sel16"]
    yT = env["yT"]; xn_hbm = env["xn_hbm"]; at_hbm = env["at_hbm"]; rec_hbm = env["rec_hbm"]
    s_cst = env["s_cst"]; s_w = env["s_w"]; s_wi = env["s_wi"]; s_wop = env["s_wop"]
    s_wo = env["s_wo"]; s_cld = env["s_cld"]; s_xt = env["s_xt"]; s_xnin = env["s_xnin"]
    s_att = env["s_att"]; s_xc = env["s_xc"]; s_xnst = env["s_xnst"]
    s_ast = env["s_ast"]; s_yst = env["s_yst"]
    s_pe = env["s_pe"]; s_sc = env["s_sc"]; s_ve = env["s_ve"]; s_rec = env["s_rec"]
    s_sq = env["s_sq"]
    s_pei = env["s_pei"]; s_sci = env["s_sci"]; s_vei = env["s_vei"]
    s_wis = env["s_wis"]; s_hst = env["s_hst"]; s_hld = env["s_hld"]
    h_hbm = env["h_hbm"]; s_pe2 = env["s_pe2"]; s_wos = env["s_wos"]

    A.alloc("xt0", TTW * KC * 4)
    A.alloc("xt1", TTW * KC * 4)
    A.alloc("xt2", TTW * KC * 4)
    A.alloc("xsq0", TTW * KC * 2)
    A.alloc("xsq1", TTW * KC * 2)
    A.alloc("tmpA0", TTW * 4)
    A.alloc("tmpA1", TTW * 4)
    A.alloc("mrow0", TTW * 4, )
    A.alloc("mrow1", TTW * 4)
    A.alloc("rrow0", TTW * 4)
    A.alloc("rrow1", TTW * 4)
    A.alloc("exrow0", TTW * 4)
    A.alloc("exrow1", TTW * 4)
    A.alloc("var0", TTW * 4)
    A.alloc("var1", TTW * 4)
    A.alloc("w0", KC * D * 2)
    A.alloc("w1", KC * D * 2)
    A.alloc("qb0", QBW * 2)
    A.alloc("qb1", QBW * 2)
    A.alloc("t1s0", QBW * 2)
    A.alloc("t1s1", QBW * 2)
    A.alloc("t2s0", QBW * 2)
    A.alloc("t2s1", QBW * 2)

    def xt(i):
        return A.f32(f"xt{i % 3}", P, KC * TTW)

    @blk.gpsimd
    def _(g):
        g.memset(A.f32("cst", P, 1, col_off=0), EXP_SCALE)
        g.memset(A.f32("cst", P, 1, col_off=1), EXP_SHIFT)
        g.memset(A.f32("cst", P, 1, col_off=2), LN_EPS)
        g.memset(A.f32("cst", P, 1, col_off=3), -0.5)
        g.memset(A.f32("cst", P, 1, col_off=4), -1.0)
        g.memset(A.bf("onesrb", 1, P), 1.0)
        g.memset(A.f32("onescf", P, 1), 1.0)

        g.memset(A.bf("onescb", P, 1), 1.0)
        g.dma_start(out=A.f32("bias4", P, 32), in_=bias4[:, :]).then_inc(s_cst, 16)
        g.dma_start(out=A.f32("tkv", P, 2), in_=tkv[:, :]).then_inc(s_cst, 16)
        g.dma_start(out=A.bf("swp", P, P), in_=swp[:, :]).then_inc(s_cst, 16)
        g.dma_start(out=A.f32("onesrf", 1, P), in_=onesrf[:, :]).then_inc(s_cst, 16)
        g.memset(A.bf("sel16", P, KC * P), 0.0)
        g.dma_start(out=A.bf("sel16", RANK, KC * P), in_=sel16[:, :]).then_inc(s_cst, 16)
        g.dma_start(out=A.bf("bvrow", 1, D), in_=bvrow[:, :]).then_inc(s_cst, 16)
        g.dma_start(out=A.bf("tabs", P, 2 * T), in_=tabs[:, :]).then_inc(s_cst, 16)
        g.dma_start(out=A.bf("w0", P, KC * D), in_=wqL[:, :]).then_inc(s_w[0], 16)
        g.dma_start(out=A.bf("w1", P, KC * D), in_=wkL[:, :]).then_inc(s_w[1], 16)

    pe0, sc0, ve0 = C["pe"], C["sc"], C["ve"]

    @blk.sync
    def _(sy):
        for tt in range(NTT):
            if tt >= 3:
                sy.wait_ge(s_ve, ve0 + [3, 5, 7, 8][tt - 3])
            src = bass.AP(xR, tt * TTW, [[KC * T, P], [T, KC], [1, TTW]])
            sy.dma_start(out=xt(tt), in_=src).then_inc(s_xt[tt % 2], 16)

    @blk.gpsimd
    def _(g):
        for tt in range(NTT):
            g.wait_ge(s_xt[tt % 2], 16 * (tt // 2 + 1))
            if tt >= 2:
                g.wait_ge(s_pe, pe0 + 2 * (tt - 2) + 1)
            for hf in range(2):
                g.tensor_tensor(A.bf(f"xsq{tt % 2}", P, 4 * TTW, col_off=hf * 4 * TTW),
                                A.f32(f"xt{tt % 3}", P, 4 * TTW, col_off=hf * 4 * TTW),
                                A.f32(f"xt{tt % 3}", P, 4 * TTW, col_off=hf * 4 * TTW),
                                ALU.mult).then_inc(s_sq, 1)

    def sc_rstd(tt):
        return sc0 + tt + 1

    @blk.scalar
    def _(sc):
        for tt in range(NTT):
            sc.wait_ge(s_ve, ve0 + [1, 2, 4, 6][tt])
            var = A.f32(f"var{tt % 2}", 1, TTW)
            sc.activation(var, var, AF.Ln, bias=A.f32("cst", 1, 1, col_off=2))
            sc.activation(A.f32(f"rrow{tt % 2}", 1, TTW), var, AF.Exp,
                          scale=A.f32("cst", 1, 1, col_off=3)).then_inc(s_sc, 1)
    C["sc"] = sc0 + 4

    with (
        nc.psum_tensor("ps_sum0", [1, TTW], F32) as ps_sum0,
        nc.psum_tensor("ps_sum1", [1, TTW], F32) as ps_sum1,
        nc.psum_tensor("ps_sq0", [1, TTW], F32) as ps_sq0,
        nc.psum_tensor("ps_sq1", [1, TTW], F32) as ps_sq1,
        nc.psum_tensor("ps_m0", [P, TTW], F32) as ps_m0,
        nc.psum_tensor("ps_m1", [P, TTW], F32) as ps_m1,
        nc.psum_tensor("ps_r0", [P, TTW], F32) as ps_r0,
        nc.psum_tensor("ps_r1", [P, TTW], F32) as ps_r1,
    ):
        ps_sum = [ps_sum0, ps_sum1]
        ps_sq = [ps_sq0, ps_sq1]
        ps_m = [ps_m0, ps_m1]
        ps_r = [ps_r0, ps_r1]

        @blk.tensor
        def _(t):
            t.wait_ge(s_cst, 16 * 7)
            for tt in range(NTT):
                if tt >= 2:
                    t.wait_ge(s_ve, ve0 + [1, 2, 4, 6][tt - 2])
                onesc = A.bf("onescb", P, 1)
                t.wait_ge(s_xt[tt % 2], 16 * (tt // 2 + 1))
                for kc in range(KC):
                    t.matmul(ps_sum[tt % 2][0:1, :], A.f32("onescf", P, 1),
                             A.f32(f"xt{tt % 3}", P, TTW, col_off=kc * TTW),
                             start=(kc == 0), stop=(kc == KC - 1))
                for kc in range(KC):
                    if kc % 4 == 0:
                        t.wait_ge(s_sq, 2 * tt + kc // 4 + 1)
                    ins = t.matmul(ps_sq[tt % 2][0:1, :], onesc,
                                   A.bf(f"xsq{tt % 2}", P, TTW, col_off=kc * TTW),
                                   start=(kc == 0), stop=(kc == KC - 1))
                ins.then_inc(s_pe, 1)
                t.wait_ge(s_sc, sc_rstd(tt))
                if tt >= 2:
                    t.wait_ge(s_ve, ve0 + [3, 5, 7, 8][tt - 2])
                onr = A.f32("onesrf", 1, P)
                t.matmul(ps_m[tt % 2][:, :], onr, A.f32(f"mrow{tt % 2}", 1, TTW),
                         start=True, stop=True)
                t.matmul(ps_r[tt % 2][:, :], onr, A.f32(f"rrow{tt % 2}", 1, TTW),
                         start=True, stop=True).then_inc(s_pe, 1)
        C["pe"] = pe0 + 8

        VE_OPS = [1, 2, 4, 6]
        VE_XN = [3, 5, 7, 8]

        @blk.vector
        def _(v):
            def stats_ops(tt):
                v.wait_ge(s_pe, pe0 + 2 * tt + 1)
                mr = A.f32(f"mrow{tt % 2}", 1, TTW)
                ex = A.f32(f"exrow{tt % 2}", 1, TTW)
                var = A.f32(f"var{tt % 2}", 1, TTW)
                v.tensor_scalar_mul(mr, ps_sum[tt % 2][0:1, :], 1.0 / D)
                v.tensor_scalar_mul(ex, ps_sq[tt % 2][0:1, :], 1.0 / D)
                v.tensor_tensor(var, mr, mr, ALU.mult)
                v.tensor_tensor(var, ex, var, ALU.subtract).then_inc(s_ve, 1)

            def xn_ops(tt):
                v.wait_ge(s_pe, pe0 + 2 * tt + 2)
                v.wait_ge(s_xt[tt % 2], 16 * (tt // 2 + 1))
                for kc in range(KC):
                    tmp = A.f32(f"tmpA{kc % 2}", P, TTW)
                    v.tensor_tensor(tmp, A.f32(f"xt{tt % 3}", P, TTW, col_off=kc * TTW),
                                    ps_m[tt % 2][:, :], ALU.subtract)
                    ins = v.tensor_tensor(A.bf("xn", P, TTW, col_off=kc * T + tt * TTW),
                                          tmp, ps_r[tt % 2][:, :], ALU.mult)
                ins.then_inc(s_ve, 1)

            stats_ops(0); stats_ops(1); xn_ops(0); stats_ops(2)
            xn_ops(1); stats_ops(3); xn_ops(2); xn_ops(3)
        C["ve"] = ve0 + 8

    @blk.sync
    def _(sy):
        sy.wait_ge(s_ve, ve0 + 8)
        sy.dma_start(out=xn_hbm[:, :], in_=A.bf("xn", P, KC * T)).then_inc(s_xnst, 16)

    if PHASES < 2:
        @blk.sync
        def _(sy):
            sy.wait_ge(s_xnst, 16)
        return
    A.free("xt0"); A.free("xt1"); A.free("xt2")
    A.free("xsq0"); A.free("xsq1"); A.free("tmpA0"); A.free("tmpA1")
    for nm in ["mrow0", "mrow1", "rrow0", "rrow1", "exrow0", "exrow1", "var0", "var1"]:
        A.free(nm)
    A.alloc("qrope", KC * T * 2)
    A.alloc("krope", KC * T * 2)
    A.alloc("vst", NMC * VROW * 2)

    pe0, sc0, ve0 = C["pe"], C["sc"], C["ve"]

    @blk.gpsimd
    def _(g):
        g.wait_ge(s_ve, ve0)
        g.wait_ge(s_xt[0], 32)
        g.wait_ge(s_xt[1], 32)
        g.memset(A.bf("vst", P, NMC * VROW), 1.0)
        g.wait_ge(s_pe, pe0 + 2 * 16)
        g.dma_start(out=A.bf("w0", P, KC * D), in_=wvL[:, :]).then_inc(s_w[0], 16)

    rope_dst = {0: "qrope", 1: "krope"}

    with (
        nc.psum_tensor("ps_qk0", [P, QBW], F32) as ps_qk0,
        nc.psum_tensor("ps_qk1", [P, QBW], F32) as ps_qk1,
        nc.psum_tensor("ps_qs0", [P, QBW], F32) as ps_qs0,
        nc.psum_tensor("ps_qs1", [P, QBW], F32) as ps_qs1,
    ):
        ps_qk = [ps_qk0, ps_qk1]
        ps_qs = [ps_qs0, ps_qs1]

        def units():
            u = 0
            for p in range(2):
                for oc in range(KC):
                    for th in range(NQB):
                        yield u, p, oc, th
                        u += 1

        @blk.tensor
        def _(t):
            t.wait_ge(s_w[0], 16)
            t.wait_ge(s_w[1], 16)
            t.wait_ge(s_ve, ve0)
            def swap_mm(u):
                t.wait_ge(s_sc, sc0 + u + 1)
                if u >= 2:
                    t.wait_ge(s_ve, ve0 + (u - 2) + 1)
                for half in range(2):
                    ins = t.matmul(ps_qs[u % 2][:, half * 512:(half + 1) * 512],
                                   A.bf("swp", P, P),
                                   A.bf(f"qb{u % 2}", P, 512, col_off=half * 512),
                                   start=True, stop=True)
                ins.then_inc(s_pe, 1)

            for u, p, oc, th in units():
                if u >= 2:
                    t.wait_ge(s_sc, sc0 + (u - 2) + 1)
                wslot = f"w{p}"
                for kc in range(KC):
                    for half in range(2):
                        ins = t.matmul(
                            ps_qk[u % 2][:, half * 512:(half + 1) * 512],
                            A.bf(wslot, P, P, col_off=kc * D + oc * P),
                            A.bf("xn", P, 512,
                                 col_off=kc * T + th * QBW + half * 512),
                            start=(kc == 0), stop=(kc == KC - 1))
                ins.then_inc(s_pe, 1)
                if u >= 1:
                    swap_mm(u - 1)
            swap_mm(31)
            t.wait_ge(s_w[0], 32)
            t.wait_ge(s_sc, sc0 + 32)
            for m in range(NMC):
                if m >= 2:
                    t.wait_ge(s_sc, sc0 + 32 + (m - 2) + 1)
                for kc in range(KC):
                    for half in range(2):
                        t.matmul(ps_qk[m % 2][:, half * 512:(half + 1) * 512],
                                 A.bf("xn", P, P, col_off=kc * T + m * P),
                                 A.bf("w0", P, 512, col_off=kc * D + half * 512),
                                 start=(kc == 0), stop=False)
                for half in range(2):
                    ins = t.matmul(ps_qk[m % 2][:, half * 512:(half + 1) * 512],
                                   A.bf("onesrb", 1, P),
                                   A.bf("bvrow", 1, 512, col_off=half * 512),
                                   start=False, stop=True)
                ins.then_inc(s_pe, 1)
        C["pe"] = pe0 + 2 * 32 + 16

        @blk.scalar
        def _(sc):
            for u, p, oc, th in units():
                sc.wait_ge(s_pe, pe0 + (1 if u == 0 else 2 * u))
                if u >= 2:
                    sc.wait_ge(s_ve, ve0 + (u - 2) + 1)
                sc.activation(A.bf(f"qb{u % 2}", P, QBW), ps_qk[u % 2][:, :], AF.Identity,
                              bias=A.f32("bias4", P, 1, col_off=p * 8 + oc)).then_inc(s_sc, 1)
            for m in range(NMC):
                sc.wait_ge(s_pe, pe0 + 64 + m + 1)
                voff = A.live["vst"][0] // 2 + m * VROW
                dstap = bass.AP(A.hbf, voff, [[A.total // 2, P], [65, NH], [1, DH]])
                srcap = bass.AP(ps_qk[m % 2], 0, [[QBW, P], [DH, NH], [1, DH]])
                sc.activation(dstap, srcap, AF.Copy).then_inc(s_sc, 1)
        C["sc"] = sc0 + 32 + 16

        @blk.vector
        def _(v):
            for u, p, oc, th in units():
                v.wait_ge(s_sc, sc0 + u + 1)
                t1 = A.bf(f"t1s{u % 2}", P, QBW)
                t2 = A.bf(f"t2s{u % 2}", P, QBW)
                v.tensor_tensor(t1, A.bf(f"qb{u % 2}", P, QBW),
                                A.bf("tabs", P, QBW, col_off=th * QBW), ALU.mult)
                v.wait_ge(s_pe, pe0 + (2 * u + 3 if u < 31 else 64))
                v.tensor_tensor(t2, ps_qs[u % 2][:, :],
                                A.bf("tabs", P, QBW, col_off=T + th * QBW), ALU.mult)
                v.tensor_tensor(A.bf(rope_dst[p], P, QBW, col_off=oc * T + th * QBW),
                                t1, t2, ALU.add).then_inc(s_ve, 1)
        C["ve"] = ve0 + 32

    if PHASES < 3:
        @blk.sync
        def _(sy):
            sy.wait_ge(s_xnst, 16)
        return
    peA, scA, veA = C["pe"], C["sc"], C["ve"]
    A.free("w0"); A.free("w1")
    A.free("qb0"); A.free("qb1"); A.free("t1s0"); A.free("t1s1")
    A.free("t2s0"); A.free("t2s1")
    A.free("xn")
    A.free("tabs")
    A.free("swp")
    A.free("onesrf")
    A.free("bvrow")
    A.alloc("xnC0", KC * TTW * 2)
    A.alloc("xnC1", KC * TTW * 2)
    A.alloc("wis0", KC * P * 2)
    A.alloc("wis1", KC * P * 2)
    A.alloc("a01b", 2 * P * 2)
    A.alloc("b0p", FF * 2)
    A.alloc("b1b", FF * 2)
    A.alloc("xas0", TTW * 2)
    A.alloc("xas1", TTW * 2)
    A.alloc("xas0n", TTW * 2)
    A.alloc("h0s0", TTW * 2)
    A.alloc("h0s1", TTW * 2)
    A.alloc("h1s0", TTW * 2)
    A.alloc("h1s1", TTW * 2)
    A.alloc("hch0", TTW * 2)
    A.alloc("hch1", TTW * 2)
    A.alloc("hC0r", FC * TTW * 2)
    A.alloc("attn0", T * 2)
    A.alloc("attn1", T * 2)
    A.alloc("pt0", QBW * 2)
    A.alloc("pt1", QBW * 2)
    A.alloc("sumsb", QBW * 4)
    A.alloc("recf", 512 * 4)

    pei_pos = {}
    _k = 0
    for _tt in range(NTT):
        _k += 1; pei_pos[(_tt, "xA", 0)] = _k
        _k += 1; pei_pos[(_tt, "xA", 1)] = _k
        for _fc in range(FC):
            _k += 1; pei_pos[(_tt, "I", _fc)] = _k
            if _fc >= 1:
                _k += 1; pei_pos[(_tt, "D", _fc - 1)] = _k
        _k += 1; pei_pos[(_tt, "D", FC - 1)] = _k
    PEI_TOT = _k
    vei_pos = {}
    _k = 0
    for _tt in range(NTT):
        for _fc in range(FC):
            _k += 1; vei_pos[(_tt, "h0", _fc)] = _k
            if _fc >= 1:
                _k += 1; vei_pos[(_tt, "h1", _fc - 1)] = _k
                _k += 1; vei_pos[(_tt, "hadd", _fc - 1)] = _k
        _k += 1; vei_pos[(_tt, "h1", FC - 1)] = _k
        _k += 1; vei_pos[(_tt, "hadd", FC - 1)] = _k
    VEI_TOT = _k

    @blk.gpsimd
    def _(g):
        g.wait_ge(s_pe, peA)
        g.wait_ge(s_ve, veA)
        g.wait_ge(s_xnst, 16)
        g.memset(A.bf("xas0", P, TTW), 0.0)
        g.memset(A.bf("xas1", P, TTW), 0.0)
        g.memset(A.bf("xas0n", P, TTW), 0.0)
        g.memset(A.bf("b0p", P, FF), 0.0)
        g.memset(A.bf("b1b", P, FF), 0.0)
        g.dma_start(out=A.bf("a01b", P, 2 * P), in_=a01[:, :]).then_inc(s_cld, 16)
        g.dma_start(out=A.bf("b0p", RANK, FF), in_=b0T[:, :]).then_inc(s_cld, 16)
        g.dma_start(out=A.bf("b1b", RANK, FF), in_=b1T[:, :]).then_inc(s_cld, 16)
        for tt in range(2):
            srcx = bass.AP(xn_hbm, tt * TTW, [[KC * T, P], [T, KC], [1, TTW]])
            g.dma_start(out=A.bf(f"xnC{tt % 2}", P, KC * TTW),
                        in_=srcx).then_inc(s_xnin, 16)
        for idx in range(NTT * FC):
            tt, fc = divmod(idx, FC)
            if fc == 0 and tt >= 2:
                g.wait_ge(s_pei, pei_pos[(tt - 2, "D", FC - 1)])
                srcx = bass.AP(xn_hbm, tt * TTW, [[KC * T, P], [T, KC], [1, TTW]])
                g.dma_start(out=A.bf(f"xnC{tt % 2}", P, KC * TTW),
                            in_=srcx).then_inc(s_xnin, 16)
            if idx >= 2:
                tp, fp = divmod(idx - 2, FC)
                g.wait_ge(s_pei, pei_pos[(tp, "I", fp)])
            srcw = bass.AP(wiL, fc * P, [[KC * FF, P], [FF, KC], [1, P]])
            g.dma_start(out=A.bf(f"wis{idx % 2}", P, KC * P),
                        in_=srcw).then_inc(s_wis[idx % 2], 16)

    pe0, sc0, ve0 = peA, scA, veA
    QW = 512

    def pe_S_i(i):
        return pe0 + i + 1

    inter_pe = []

    with (
        nc.psum_tensor("ps_S0", [P, QBW], F32) as ps_S0,
        nc.psum_tensor("ps_S1", [P, QBW], F32) as ps_S1,
        nc.psum_tensor("ps_Oa", [65, QW], F32) as ps_Oa,
        nc.psum_tensor("ps_Ob", [65, QW], F32) as ps_Ob,
        nc.psum_tensor("ps_I0", [P, TTW], F32) as ps_I0,
        nc.psum_tensor("ps_I1", [P, TTW], F32) as ps_I1,
    ):
        ps_S = [ps_S0, ps_S1]
        ps_O = [ps_Oa, ps_Ob]
        ps_I = [ps_I0, ps_I1]

        def mk_xA(tt, e):
            def emit_half(hf):
                def emit(t):
                    if hf == 0:
                        if e == 0:
                            t.wait_ge(s_xnin, 16 * (tt + 1))
                        if tt >= 1:
                            t.wait_ge(s_vei, vei_pos[(tt - 1, "h1", FC - 2 + e)])
                    ins = None
                    for kc in range(hf * 4, hf * 4 + 4):
                        ins = t.matmul(ps_I[e][0:RANK, :],
                                       A.bf("a01b", P, RANK, col_off=e * P + kc * RANK),
                                       A.bf(f"xnC{tt % 2}", P, TTW, col_off=kc * TTW),
                                       start=(kc == 0), stop=(kc == KC - 1))
                    if hf == 1:
                        ins.then_inc(s_pei, 1)
                return emit
            inter_pe.append((4, (tt, "xA", e, 0), emit_half(0)))
            inter_pe.append((4, (tt, "xA", e), emit_half(1)))

        def mk_I(tt, fc):
            idx = tt * FC + fc
            def emit_half(hf):
                def emit(t):
                    if hf == 0:
                        t.wait_ge(s_wis[idx % 2], 16 * (idx // 2 + 1))
                        if fc == 0:
                            t.wait_ge(s_sci, tt * 3 + 1)
                        elif fc == 1:
                            t.wait_ge(s_sci, tt * 3 + 2)
                        if fc >= 2:
                            t.wait_ge(s_vei, vei_pos[(tt, "h1", fc - 2)])
                    for kc in range(hf * 4, hf * 4 + 4):
                        t.matmul(ps_I[fc % 2][:, :],
                                 A.bf(f"wis{idx % 2}", P, P, col_off=kc * P),
                                 A.bf(f"xnC{tt % 2}", P, TTW, col_off=kc * TTW),
                                 start=(kc == 0), stop=False)
                    if hf == 1:
                        t.matmul(ps_I[fc % 2][:, :],
                                 A.bf("b0p", P, P, col_off=fc * P),
                                 A.bf("xas0", P, TTW),
                                 start=False, stop=True).then_inc(s_pei, 1)
                return emit
            inter_pe.append((4, (tt, "I", fc, 0), emit_half(0)))
            inter_pe.append((5, (tt, "I", fc), emit_half(1)))

        def mk_D(tt, fc):
            def emit(t):
                t.wait_ge(s_vei, vei_pos[(tt, "h0", fc)])
                if fc == 0:
                    t.wait_ge(s_sci, tt * 3 + 3)
                t.matmul(ps_I[fc % 2][:, :],
                         A.bf("b1b", P, P, col_off=fc * P),
                         A.bf("xas1", P, TTW), start=False, stop=False,
                         skip_group_check=True)
                t.matmul(ps_I[fc % 2][:, :],
                         A.bf("b0p", P, P, col_off=fc * P),
                         A.bf("xas0n", P, TTW),
                         start=False, stop=True,
                         skip_group_check=True).then_inc(s_pei, 1)
            inter_pe.append((2, (tt, "D", fc), emit))

        for tt in range(NTT):
            mk_xA(tt, 0)
            mk_xA(tt, 1)
            for fc in range(FC):
                mk_I(tt, fc)
                if fc >= 1:
                    mk_D(tt, fc - 1)
            mk_D(tt, FC - 1)
        INTER_MM = sum(n for n, _, _ in inter_pe)
        step_of = {}

        @blk.tensor
        def _(t):
            t.wait_ge(s_ve, veA)
            t.wait_ge(s_sc, scA)
            t.wait_ge(s_cld, 48)

            def S_pair(i, inc):
                g, m = divmod(i, 16)
                c, qb4 = divmod(g, 4)
                ins = None
                for hh in range(2):
                    ins = t.matmul(
                        ps_S[i % 2][:, hh * QW:(hh + 1) * QW],
                        A.bf("krope", DH, P, row_off=hh * 64, col_off=c * T + m * P),
                        A.bf("qrope", DH, QW, row_off=hh * 64,
                             col_off=c * T + qb4 * QW),
                        start=True, stop=True)
                if inc:
                    ins.then_inc(s_pe2, 1)
                return ins

            for i in range(2):
                S_pair(i, True)
            iop = 0
            emitted = 0
            for i in range(512):
                g, m = divmod(i, 16)
                c, qb4 = divmod(g, 4)
                t.wait_ge(s_sc, sc0 + i + 1)
                if m == 0 and g >= 1:
                    t.wait_ge(s_ve, ve0 + 2 * g - 1)
                if i + 2 < 512:
                    S_pair(i + 2, True)
                for hh in range(2):
                    ins = t.matmul(
                        ps_O[hh][:, :],
                        A.bf("vst", P, 65, col_off=m * VROW + (2 * c + hh) * 65),
                        A.bf(f"pt{i % 2}", P, QW, col_off=hh * QW),
                        start=(m == 0), stop=(m == NMC - 1))
                ins.then_inc(s_pe, 1)
                target = (i + 1) * INTER_MM / 512.0
                while iop < len(inter_pe) and emitted < target:
                    n, key, fn = inter_pe[iop]
                    fn(t)
                    step_of[key] = i
                    emitted += n
                    iop += 1
            while iop < len(inter_pe):
                n, key, fn = inter_pe[iop]
                fn(t)
                step_of[key] = 511
                iop += 1
        C["pe"] = pe0 + 512

        def step_c(i):
            return divmod(divmod(i, 16)[0], 4)[0]

        sc_inserts = {}
        for tt in range(NTT):
            st = min(step_of[(tt, "xA", 1)] + 2, 511)
            def mk_xas(tt):
                def emit(sc):
                    sc.wait_ge(s_pei, pei_pos[(tt, "xA", 0)])
                    if tt >= 1:
                        sc.wait_ge(s_pei, pei_pos[(tt - 1, "I", FC - 1)])
                    sc.activation(A.bf("xas0", RANK, TTW), ps_I[0][0:RANK, :],
                                  AF.Copy).then_inc(s_sci, 1)
                    sc.wait_ge(s_pei, pei_pos[(tt, "xA", 1)])
                    if tt >= 1:
                        sc.wait_ge(s_pei, pei_pos[(tt - 1, "D", FC - 1)])
                    sc.activation(A.bf("xas1", RANK, TTW), ps_I[1][0:RANK, :],
                                  AF.Copy).then_inc(s_sci, 1)
                    sc.activation(A.bf("xas0n", RANK, TTW), A.bf("xas0", RANK, TTW),
                                  AF.Identity,
                                  scale=A.f32("cst", RANK, 1, col_off=4)).then_inc(s_sci, 1)
                return emit
            sc_inserts.setdefault(st, []).append(mk_xas(tt))

        @blk.scalar
        def _(sc):
            for i in range(512):
                sc.wait_ge(s_pe2, i + 1)
                if i >= 2:
                    sc.wait_ge(s_pe, pe0 + i - 1)
                sc.activation(A.bf(f"pt{i % 2}", P, QBW), ps_S[i % 2][:, :],
                              AF.Exp, bias=A.f32("cst", P, 1, col_off=1),
                              scale=A.f32("cst", P, 1, col_off=0)).then_inc(s_sc, 1)
                for fn in sc_inserts.get(i, []):
                    fn(sc)
        C["sc"] = sc0 + 512

        ve_inserts = {}

        def add_ve(st, emit):
            ve_inserts.setdefault(min(st, 511), []).append(emit)

        def mk_h0(tt, fc):
            def emit(v):
                v.wait_ge(s_pei, pei_pos[(tt, "I", fc)])
                v.tensor_scalar(A.bf(f"h0s{fc % 2}", P, TTW), ps_I[fc % 2][:, :],
                                A.f32("tkv", P, 1, col_off=0), 0.0,
                                ALU.mult, ALU.max).then_inc(s_vei, 1)
            return emit

        def mk_h1(tt, fc):
            def emit(v):
                v.wait_ge(s_pei, pei_pos[(tt, "D", fc)])
                v.tensor_scalar(A.bf(f"h1s{fc % 2}", P, TTW), ps_I[fc % 2][:, :],
                                A.f32("tkv", P, 1, col_off=1), 0.0,
                                ALU.mult, ALU.max).then_inc(s_vei, 1)
            return emit

        def mk_hadd(tt, fc):
            j = (tt - 1) * FC + fc
            def emit(v):
                if tt == 0:
                    dst = A.bf("hC0r", P, TTW, col_off=fc * TTW)
                else:
                    if j >= 2:
                        v.wait_ge(s_hst[j % 2], 16 * (j // 2))
                    dst = A.bf(f"hch{j % 2}", P, TTW)
                v.tensor_tensor(dst,
                                A.bf(f"h0s{fc % 2}", P, TTW),
                                A.bf(f"h1s{fc % 2}", P, TTW),
                                ALU.add).then_inc(s_vei, 1)
            return emit

        for tt in range(NTT):
            for fc in range(FC):
                add_ve(step_of[(tt, "I", fc)], mk_h0(tt, fc))
                if fc >= 1:
                    add_ve(step_of[(tt, "D", fc - 1)], mk_h1(tt, fc - 1))
                    add_ve(step_of[(tt, "D", fc - 1)], mk_hadd(tt, fc - 1))
            add_ve(step_of[(tt, "D", FC - 1)], mk_h1(tt, FC - 1))
            add_ve(step_of[(tt, "D", FC - 1)], mk_hadd(tt, FC - 1))

        @blk.vector
        def _(v):
            for i in range(512):
                g, m = divmod(i, 16)
                if m == 15:
                    c, qb4 = divmod(g, 4)
                    v.wait_ge(s_pe, pe0 + g * 16 + 16)
                    if c >= 2 and qb4 == 0:
                        v.wait_ge(s_ast[c % 2], 16 * (c // 2))
                    for hh in range(2):
                        v.tensor_copy(
                            A.bf(f"attn{c % 2}", 64, QW, row_off=hh * 64,
                                 col_off=qb4 * QW),
                            ps_O[hh][0:64, :])
                    ins = None
                    for hh in range(2):
                        ins = v.tensor_copy(A.f32("sumsb", 1, QW, col_off=hh * QW),
                                            ps_O[hh][64:65, :])
                    ins.then_inc(s_ve, 1)
                    if g >= 1:
                        v.wait_ge(s_rec, 16 * 2 * g)
                    for hh in range(2):
                        ins = v.tensor_copy(A.bf("recf", 1, QW, col_off=hh * QW),
                                            A.f32("sumsb", 1, QW, col_off=hh * QW))
                    ins.then_inc(s_ve, 1)
                for fn in ve_inserts.get(i, []):
                    fn(v)
        C["ve"] = ve0 + 2 * 32

        sy_inserts = {}
        for tt in range(1, NTT):
            for fc in range(FC):
                j = (tt - 1) * FC + fc
                def mk_hst(tt, fc, j):
                    def emit(sy):
                        sy.wait_ge(s_vei, vei_pos[(tt, "hadd", fc)])
                        sy.dma_start(
                            out=h_hbm[:, fc * T + tt * TTW: fc * T + tt * TTW + TTW],
                            in_=A.bf(f"hch{j % 2}", P, TTW)
                        ).then_inc(s_hst[j % 2], 16)
                    return emit
                st = min(step_of[(tt, "D", fc)] + 2, 511)
                sy_inserts.setdefault(st, []).append(mk_hst(tt, fc, j))

        @blk.sync
        def _(sy):
            for i in range(512):
                g, m = divmod(i, 16)
                if m == 15:
                    c, qb4 = divmod(g, 4)
                    sy.wait_ge(s_ve, ve0 + 2 * g + 2)
                    for hh in range(2):
                        sy.dma_start(
                            out=rec_hbm[2 * c + hh:2 * c + hh + 1,
                                        qb4 * QW:(qb4 + 1) * QW],
                            in_=A.bf("recf", 1, QW, col_off=hh * QW)
                        ).then_inc(s_rec, 16)
                    if qb4 == 3:
                        sy.wait_ge(s_ve, ve0 + 2 * g + 1)
                        sy.dma_start(out=env["at_hbm"][:, c * T:(c + 1) * T],
                                     in_=A.bf(f"attn{c % 2}", P, T)
                                     ).then_inc(s_ast[c % 2], 16)
                for fn in sy_inserts.get(i, []):
                    fn(sy)

    if PHASES < 4:
        @blk.sync
        def _(sy):
            sy.wait_ge(s_ast[0], 64)
            sy.wait_ge(s_ast[1], 64)
            sy.wait_ge(s_hst[0], 16 * 16 * (NTT - 1))
            sy.wait_ge(s_hst[1], 16 * 16 * (NTT - 1))
        return
    peB2, scB2, veB2 = C["pe"], C["sc"], C["ve"]
    A.free("qrope"); A.free("krope"); A.free("vst")
    A.free("pt0"); A.free("pt1")
    A.free("sumsb"); A.free("recf")
    A.free("attn0"); A.free("attn1")
    A.free("xnC0"); A.free("xnC1"); A.free("wis0"); A.free("wis1")
    A.free("a01b"); A.free("b0p"); A.free("b1b")
    A.free("xas0"); A.free("xas1"); A.free("xas0n")
    A.free("h0s0"); A.free("h0s1"); A.free("h1s0"); A.free("h1s1")
    A.free("hch0"); A.free("hch1")
    A.alloc("rec_all", T * 2)
    A.alloc("recallf", T * 4)
    A.alloc("woP", KC * D * 2)
    A.alloc("hC0", FC * TTW * 2)
    A.alloc("hC1", FC * TTW * 2)
    A.alloc("wos0", FF * 2)
    A.alloc("wos1", FF * 2)
    A.alloc("wos2", FF * 2)
    A.alloc("atC0", KC * TTW * 2)
    A.alloc("atC1", KC * TTW * 2)
    A.alloc("xcc0", TTW * 4)
    A.alloc("xcc1", TTW * 4)
    A.alloc("outc0", TTW * 4)
    A.alloc("outc1", TTW * 4)
    A.alloc("t2s0c", TTW * 4)
    A.alloc("t2s1c", TTW * 4)

    pe_pos2 = {}
    _k = 0
    for _j in range(KC):
        _k += 1; pe_pos2[(0, "rb", _j)] = _k
    for _tt in range(NTT):
        for _j in range(KC):
            _k += 1; pe_pos2[(_tt, "F", _j)] = _k
            if _tt + 1 < NTT:
                _k += 1; pe_pos2[(_tt + 1, "rb", _j)] = _k
    PE2_TOT = _k
    ve_pos2 = {}
    _k = 2
    ve_pos2[("rec", 0)] = 2
    for _j in range(KC):
        _k += 1; ve_pos2[(0, "rbTT", _j)] = _k
    for _tt in range(NTT):
        if _tt + 1 < NTT:
            _k += 2; ve_pos2[("rec", _tt + 1)] = _k
        for _j in range(KC):
            _k += 1; ve_pos2[(_tt, "out", _j)] = _k
            if _tt + 1 < NTT:
                _k += 1; ve_pos2[(_tt + 1, "rbTT", _j)] = _k
    VE2_TOT = _k

    def peC2(tt, key, j):
        return peB2 + pe_pos2[(tt, key, j)]

    def scC2(tt, oc):
        return scB2 + tt * KC + oc + 1

    def veC2(tt, key, j):
        return veB2 + ve_pos2[(tt, key, j)]

    @blk.gpsimd
    def _(g):
        g.wait_ge(s_pe, peB2)
        g.wait_ge(s_vei, VEI_TOT)
        g.wait_ge(s_hst[0], 16 * 16 * (NTT - 1))
        g.wait_ge(s_hst[1], 16 * 16 * (NTT - 1))
        g.wait_ge(s_ast[0], 64)
        g.wait_ge(s_ast[1], 64)
        g.wait_ge(s_rec, 16 * 64)
        g.dma_start(out=A.bf("rec_all", RANK, T), in_=rec_hbm[:, :]).then_inc(s_cld, 16)
        for hf in range(2):
            g.dma_start(out=A.bf("woP", P, 4 * D, col_off=hf * 4 * D),
                        in_=woPL[:, hf * 4 * D:(hf + 1) * 4 * D]).then_inc(s_wop, 16)
        for tt in range(NTT):
            for oc in range(KC):
                idx = tt * KC + oc
                if idx >= 3:
                    t3p, o3p = divmod(idx - 3, KC)
                    g.wait_ge(s_pe, peC2(t3p, "F", o3p))
                for hf in range(2):
                    g.dma_start(out=A.bf(f"wos{idx % 3}", P, FF // 2, col_off=hf * FF // 2),
                                in_=woL[:, oc * FF + hf * FF // 2:
                                        oc * FF + (hf + 1) * FF // 2]).then_inc(s_wos, 16)

    @blk.sync
    def _(sy):
        sy.wait_ge(s_pe, peB2)
        sy.wait_ge(s_vei, VEI_TOT)
        sy.wait_ge(s_hst[0], 16 * 16 * (NTT - 1))
        sy.wait_ge(s_hst[1], 16 * 16 * (NTT - 1))
        sy.wait_ge(s_ast[0], 64)
        sy.wait_ge(s_ast[1], 64)
        sy.wait_ge(s_rec, 16 * 64)

        def loads(tt):
            if tt >= 2:
                sy.wait_ge(s_pe, peC2(tt - 2, "F", KC - 1))
            src = bass.AP(at_hbm, tt * TTW, [[KC * T, P], [T, KC], [1, TTW]])
            sy.dma_start(out=A.bf(f"atC{tt % 2}", P, KC * TTW),
                         in_=src).then_inc(s_att[tt % 2], 16)

        def loads_h(tt):
            if tt >= 3:
                sy.wait_ge(s_pe, peC2(tt - 2, "F", KC - 1))
            srch = bass.AP(h_hbm, tt * TTW, [[FC * T, P], [T, FC], [1, TTW]])
            sy.dma_start(out=A.bf(f"hC{tt % 2}", P, FC * TTW),
                         in_=srch).then_inc(s_hld, 16)

        def xcc_load(idx2):
            tt2, oc2 = divmod(idx2, KC)
            sy.dma_start(out=A.f32(f"xcc{idx2 % 2}", P, TTW),
                         in_=xR[:, oc2 * T + tt2 * TTW: oc2 * T + tt2 * TTW + TTW]
                         ).then_inc(s_xc[idx2 % 2], 16)

        def stores(tt):
            for oc in range(KC):
                idx = tt * KC + oc
                sy.wait_ge(s_ve, veC2(tt, "out", oc))
                sy.dma_start(out=yT[oc * P:(oc + 1) * P, tt * TTW:(tt + 1) * TTW],
                             in_=A.f32(f"outc{idx % 2}", P, TTW)).then_inc(s_yst[idx % 2], 16)
                if idx + 2 < NTT * KC:
                    xcc_load(idx + 2)

        xcc_load(0)
        xcc_load(1)
        loads(0)
        loads(1)
        loads_h(1)
        loads_h(2)
        for tt in range(1, NTT):
            stores(tt - 1)
            if tt + 1 < NTT:
                loads(tt + 1)
            if tt == 2:
                loads_h(3)
        stores(NTT - 1)
        sy.wait_ge(s_yst[0], 16 * 16)
        sy.wait_ge(s_yst[1], 16 * 16)

    with (
        nc.psum_tensor("ps_F0", [P, TTW], F32) as ps_F0,
        nc.psum_tensor("ps_F1", [P, TTW], F32) as ps_F1,
        nc.psum_tensor("ps_rb0", [P, TTW], F32) as ps_rb0,
        nc.psum_tensor("ps_rb1", [P, TTW], F32) as ps_rb1,
    ):
        ps_F = [ps_F0, ps_F1]
        ps_rb = [ps_rb0, ps_rb1]

        @blk.tensor
        def _(t):
            def rb_mm(tt, fc):
                if fc == 0:
                    t.wait_ge(s_ve, veB2 + ve_pos2[("rec", tt)])
                if fc >= 2:
                    t.wait_ge(s_ve, veC2(tt, "rbTT", fc - 2))
                elif tt >= 1:
                    t.wait_ge(s_ve, veC2(tt - 1, "rbTT", KC - 2 + fc))
                t.matmul(ps_rb[fc % 2][:, :], A.bf("sel16", P, P, col_off=fc * P),
                         A.bf("rec_all", P, TTW, col_off=tt * TTW),
                         start=True, stop=True).then_inc(s_pe, 1)

            for fc in range(KC):
                rb_mm(0, fc)
            for tt in range(NTT):
                t.wait_ge(s_ve, veC2(tt, "rbTT", KC - 1))
                if tt == 0:
                    t.wait_ge(s_vei, vei_pos[(0, "hadd", FC - 1)])
                    t.wait_ge(s_wop, 32)
                else:
                    t.wait_ge(s_hld, 16 * tt)
                for oc in range(KC):
                    idx = tt * KC + oc
                    if oc >= 2:
                        t.wait_ge(s_sc, scC2(tt, oc - 2))
                    elif tt >= 1:
                        t.wait_ge(s_sc, scC2(tt - 1, KC - 2 + oc))
                    t.wait_ge(s_wos, 32 * (idx + 1))
                    for kc in range(KC):
                        t.matmul(ps_F[oc % 2][:, :],
                                 A.bf("woP", P, P, col_off=kc * D + oc * P),
                                 A.bf(f"atC{tt % 2}", P, TTW, col_off=kc * TTW),
                                 start=(kc == 0), stop=False)
                    hsrc = "hC0r" if tt == 0 else f"hC{tt % 2}"
                    for fc in range(FC):
                        ins = t.matmul(ps_F[oc % 2][:, :],
                                       A.bf(f"wos{idx % 3}", P, P, col_off=fc * P),
                                       A.bf(hsrc, P, TTW, col_off=fc * TTW),
                                       start=False, stop=(fc == FC - 1))
                    ins.then_inc(s_pe, 1)
                    if tt + 1 < NTT:
                        rb_mm(tt + 1, oc)
        C["pe"] = peB2 + PE2_TOT

        @blk.scalar
        def _(sc):
            for tt in range(NTT):
                for oc in range(KC):
                    idx = tt * KC + oc
                    sc.wait_ge(s_pe, peC2(tt, "F", oc))
                    if idx >= 2:
                        t2p, o2p = divmod(idx - 2, KC)
                        sc.wait_ge(s_ve, veC2(t2p, "out", o2p))
                    sc.activation(A.f32(f"t2s{oc % 2}c", P, TTW), ps_F[oc % 2][:, :],
                                  AF.Identity,
                                  bias=A.f32("bias4", P, 1, col_off=24 + oc)).then_inc(s_sc, 1)
        C["sc"] = scB2 + NTT * KC

        @blk.vector
        def _(v):
            def rec_slice(tt):
                v.reciprocal(A.f32("recallf", RANK, TTW, col_off=tt * TTW),
                             A.bf("rec_all", RANK, TTW, col_off=tt * TTW)).then_inc(s_ve, 1)
                v.tensor_copy(A.bf("rec_all", RANK, TTW, col_off=tt * TTW),
                              A.f32("recallf", RANK, TTW, col_off=tt * TTW)).then_inc(s_ve, 1)

            def rbTT(tt, fc):
                v.wait_ge(s_pe, peC2(tt, "rb", fc))
                if fc == 0:
                    v.wait_ge(s_att[tt % 2], 16 * (tt // 2 + 1))
                v.tensor_tensor(A.bf(f"atC{tt % 2}", P, TTW, col_off=fc * TTW),
                                A.bf(f"atC{tt % 2}", P, TTW, col_off=fc * TTW),
                                ps_rb[fc % 2][:, :], ALU.mult).then_inc(s_ve, 1)

            v.wait_ge(s_cld, 64)
            rec_slice(0)
            for fc in range(KC):
                rbTT(0, fc)
            for tt in range(NTT):
                if tt + 1 < NTT:
                    rec_slice(tt + 1)
                for oc in range(KC):
                    idx = tt * KC + oc
                    v.wait_ge(s_sc, scC2(tt, oc))
                    v.wait_ge(s_xc[idx % 2], 16 * (idx // 2 + 1))
                    if idx >= 2:
                        v.wait_ge(s_yst[idx % 2], 16 * (idx // 2))
                    v.tensor_tensor(A.f32(f"outc{idx % 2}", P, TTW),
                                    A.f32(f"t2s{oc % 2}c", P, TTW),
                                    A.f32(f"xcc{idx % 2}", P, TTW), ALU.add).then_inc(s_ve, 1)
                    if tt + 1 < NTT:
                        rbTT(tt + 1, oc)
        C["ve"] = veB2 + VE2_TOT



def _rope_perm():
    perm = []
    for h in range(NH):
        perm += [h * DH + 2 * i for i in range(DH // 2)]
        perm += [h * DH + 2 * i + 1 for i in range(DH // 2)]
    return np.array(perm, dtype=np.int64)


def _chunk_tile(wT):
    n = wT.shape[1]
    return np.ascontiguousarray(wT.reshape(KC, P, n).transpose(1, 0, 2).reshape(P, KC * n))


def _tables():
    inv = 1.0 / (10000.0 ** (np.arange(0, DH, 2, dtype=np.float64) / DH))
    pos = np.arange(T, dtype=np.float64)
    th = pos[None, :] * inv[:, None]
    cos = np.cos(th).astype(np.float32)
    sin = np.sin(th).astype(np.float32)
    ccc = np.concatenate([cos, cos, cos, cos], axis=0)
    ssp = np.concatenate([-sin, sin, -sin, sin], axis=0)
    return np.ascontiguousarray(np.concatenate([ccc, ssp], axis=1))


def _sel16():
    s = np.zeros((RANK, KC * P), np.float32)
    for c in range(KC):
        for p in range(P):
            s[2 * c + (p >= 64), c * P + p] = 1.0
    return s


def _swap_mat():
    s = np.zeros((P, P), np.float32)
    for g in range(2):
        for i in range(32):
            s[g * 64 + i, g * 64 + 32 + i] = 1.0
            s[g * 64 + 32 + i, g * 64 + i] = 1.0
    return s


def make_common_inputs(Wq, bq, Wk, bk, Wv, bv, Wo, bo, wi_w, wo_w,
                       lora_As, lora_Bs, tki, tkvals):
    perm = _rope_perm()
    wq_p = Wq[perm]
    wk_p = Wk[perm]
    bq_p = bq[perm]
    bk_p = bk[perm]
    e0 = int(tki[0]); e1 = int(tki[1])
    d = {
        "wqL": _chunk_tile(np.ascontiguousarray(wq_p.T)),
        "wkL": _chunk_tile(np.ascontiguousarray(wk_p.T)),
        "wvL": _chunk_tile(np.ascontiguousarray(Wv.T)),
        "woPL": _chunk_tile(np.ascontiguousarray(Wo.T)),
        "wiL": _chunk_tile(np.ascontiguousarray(wi_w.T)),
        "woL": np.ascontiguousarray(
            wo_w.T.reshape(FC, P, KC, P).transpose(1, 2, 0, 3).reshape(P, KC * FF)),
        "a01": np.ascontiguousarray(
            np.stack([lora_As[e0].T, lora_As[e1].T], axis=0)
            .reshape(2, KC, P, RANK).transpose(2, 0, 1, 3).reshape(P, 2 * P)),
        "b0T": np.ascontiguousarray(lora_Bs[e0].T),
        "b1T": np.ascontiguousarray(lora_Bs[e1].T),
        "bias4": np.ascontiguousarray(
            np.stack([bq_p.reshape(KC, P), bk_p.reshape(KC, P),
                      bv.reshape(KC, P), bo.reshape(KC, P)], axis=0)
            .reshape(4 * KC, P).T),
        "bvrow": np.ascontiguousarray(bv[None, :]),
        "tkv": np.ascontiguousarray(np.broadcast_to(tkvals[None, :], (P, 2))),
        "tabs": _tables(),
        "swp": _swap_mat(),
        "onesrf": np.ones((1, P), np.float32),
        "sel16": _sel16(),
    }
    return {k: np.ascontiguousarray(v.astype(np.float32)) for k, v in d.items()}


def kernel(L0_attn_x, L0_attn_Wq, L0_attn_bq, L0_attn_Wk, L0_attn_bk,
           L0_attn_Wv, L0_attn_bv, L0_attn_Wo, L0_attn_bo,
           L0_ffn_wi_w, L0_ffn_wo_w, L0_ffn_lora_As, L0_ffn_lora_Bs,
           L0_ffn_top_k_indices, L0_ffn_top_k_values):
    x = np.asarray(L0_attn_x, dtype=np.float32)
    args = [np.asarray(a, dtype=np.float32) for a in
            (L0_attn_Wq, L0_attn_bq, L0_attn_Wk, L0_attn_bk, L0_attn_Wv,
             L0_attn_bv, L0_attn_Wo, L0_attn_bo, L0_ffn_wi_w, L0_ffn_wo_w,
             L0_ffn_lora_As, L0_ffn_lora_Bs)]
    tki = np.asarray(L0_ffn_top_k_indices)
    tkvals = np.asarray(L0_ffn_top_k_values, dtype=np.float32)
    (Wq, bq, Wk, bk, Wv, bv, Wo, bo, wi_w, wo_w, lora_As, lora_Bs) = args

    nc = build_core_program()
    common = make_common_inputs(Wq, bq, Wk, bk, Wv, bv, Wo, bo,
                                wi_w, wo_w, lora_As, lora_Bs, tki, tkvals)
    in_maps = []
    for c in range(8):
        m = dict(common)
        m["xR"] = _chunk_tile(np.ascontiguousarray(x[c].T))
        in_maps.append(m)
    res = run_bass_kernel_spmd(nc, in_maps, core_ids=list(range(8)))
    out = np.empty((8, T, D), np.float32)
    for c in range(8):
        out[c] = res.results[c]["yT"].T
    return out
